# version 02
# baseline (speedup 1.0000x reference)
import math
import numpy as np
import ml_dtypes
import concourse.bass as bass
import concourse.bacc as bacc
import concourse.mybir as mybir
from concourse.bass_isa import ReduceOp
from concourse.bass_utils import run_bass_kernel_spmd

F32 = mybir.dt.float32
BF16 = mybir.dt.bfloat16
I8 = mybir.dt.int8
ALU = mybir.AluOpType
ACT = mybir.ActivationFunctionType
AXX = mybir.AxisListType.X

NCORES = 8
D = 2048
T = 2048
NSEQ = 16384
C2 = 16
C4 = 32
SB = 256
NSTRIPE = T // SB
H = 16
EPS_ATTN = 1e-6
EPS_LN = 1e-5
EPS_Q = 1e-5
QB = 127.0
RT2048 = math.sqrt(2048.0)
RT4096 = math.sqrt(4096.0)

ENGINES = ("tensor", "vector", "scalar", "gpsimd", "sync")

PARTS = ("ffn1", "attn", "ffn2")


class Sem:
    __slots__ = ("h", "n", "name")

    def __init__(self, handle, name):
        self.h = handle
        self.n = 0
        self.name = name


class P:
    def __init__(self, nc, block):
        self.nc = nc
        self.block = block
        self.ops = {e: [] for e in ENGINES}
        self.log = {e: [] for e in ENGINES}
        self.dma_pool = []
        self.dma_rr = 0

    def init_dma_pool(self, sems):
        self.dma_pool = sems

    def emit(self, eng, fn, waits=(), inc=None, inc_k=1):
        best = {}
        for tok in waits:
            if tok is None:
                continue
            s, v = tok
            if v > best.get(s, -1):
                best[s] = v
        for s, v in best.items():
            if v > s.n:
                raise RuntimeError(f"wait {s.name}>={v} but emitted {s.n}")
            self.ops[eng].append(("w", s, v))
            self.log[eng].append(("w", s, v))
        tok = None
        if inc is not None:
            inc.n += inc_k
            tok = (inc, inc.n)
        self.ops[eng].append(("op", fn, inc, inc_k))
        self.log[eng].append(("op", None, inc, inc_k))
        return tok

    def wait_only(self, eng, waits):
        best = {}
        for tok in waits:
            if tok is None:
                continue
            s, v = tok
            if v > best.get(s, -1):
                best[s] = v
        for s, v in best.items():
            self.ops[eng].append(("w", s, v))
            self.log[eng].append(("w", s, v))

    def dma(self, eng, out, in_, waits=(), inc=None, spread=False, **kw):
        if spread and eng == "sync":
            self.spread_rr = getattr(self, "spread_rr", 0) ^ 1
            eng = ("sync", "scalar")[self.spread_rr]
        s = self.dma_pool[self.dma_rr]
        self.dma_rr = (self.dma_rr + 1) % len(self.dma_pool)
        gate = [(s, s.n)] if s.n else []
        return self.emit(eng, lambda e: e.dma_start(out=out, in_=in_, **kw),
                         waits=list(waits) + gate, inc=s, inc_k=16)

    def flush(self):
        pend = self.ops
        self.ops = {e: [] for e in ENGINES}

        def mk(ops):
            def body(eng):
                for item in ops:
                    if item[0] == "w":
                        eng.wait_ge(item[1].h, item[2])
                    else:
                        _, fn, inc, k = item
                        ins = fn(eng)
                        if inc is not None:
                            ins.then_inc(inc.h, k)
            return body
        for e in ENGINES:
            if pend[e]:
                getattr(self.block, e)(mk(pend[e]))

    def check(self):
        pos = {e: 0 for e in ENGINES}
        cnt = {}
        progress = True
        while progress:
            progress = False
            for e in ENGINES:
                while pos[e] < len(self.log[e]):
                    item = self.log[e][pos[e]]
                    if item[0] == "w":
                        if cnt.get(item[1], 0) < item[2]:
                            break
                    else:
                        _, _, inc, k = item
                        if inc is not None:
                            cnt[inc] = cnt.get(inc, 0) + k
                    pos[e] += 1
                    progress = True
        stuck = {}
        for e in ENGINES:
            if pos[e] < len(self.log[e]):
                it = self.log[e][pos[e]]
                stuck[e] = (f"idx {pos[e]}/{len(self.log[e])} wait "
                            f"{it[1].name}>={it[2]} have {cnt.get(it[1], 0)}")
        if stuck:
            raise RuntimeError(f"schedule deadlock: {stuck}")


class SlotTracker:
    def __init__(self, n):
        self.w = [None] * n
        self.r = [[] for _ in range(n)]

    def read_waits(self, idxs):
        return [self.w[i] for i in idxs if self.w[i] is not None]

    def note_read(self, idxs, tok):
        for i in idxs:
            self.r[i].append(tok)

    def write_waits(self, idxs):
        out = []
        for i in idxs:
            if self.w[i] is not None:
                out.append(self.w[i])
            out.extend(self.r[i])
        return out

    def note_write(self, idxs, tok):
        for i in idxs:
            self.w[i] = tok
            self.r[i] = []



def fm_layout(x_tok_major):
    Tl, Dl = x_tok_major.shape
    a = x_tok_major.reshape(Tl, Dl // 128, 128).transpose(2, 1, 0)
    return np.ascontiguousarray(a.reshape(128, (Dl // 128) * Tl))


def fm_unlayout(a, Tl, Dl):
    x = a.reshape(128, Dl // 128, Tl).transpose(2, 1, 0)
    return np.ascontiguousarray(x.reshape(Tl, Dl))


def col_layout(v):
    return np.ascontiguousarray(np.asarray(v).reshape(-1, 128).T)


def _hadamard(n):
    h = np.array([[1.0]], dtype=np.float32)
    while h.shape[0] < n:
        h = np.block([[h, h], [h, -h]])
    return h.astype(np.float32)



def build_program(parts=None):
    if parts is None:
        parts = PARTS
    nc = bacc.Bacc("TRN2", target_bir_lowering=False)
    ctx = {"nc": nc, "parts": parts, "psum_free": [None] * 8}

    ctx["x_in"] = nc.dram_tensor("x_fm", [128, C2 * T], F32, kind="ExternalInput").ap()
    ctx["y_out"] = nc.dram_tensor("y_fm", [128, C2 * T], F32, kind="ExternalOutput").ap()
    ctx["wup_t"] = nc.dram_tensor("w_up_t", [D, 4096], F32, kind="ExternalInput").ap()
    ctx["wdn_t"] = nc.dram_tensor("w_down_t", [4096, D], F32, kind="ExternalInput").ap()
    for nm in ("wq_t", "wk_t", "wv_t", "wo_t"):
        ctx[nm] = nc.dram_tensor(nm, [D, D], F32, kind="ExternalInput").ap()
    ctx["bq_col_in"] = nc.dram_tensor("bq_col", [128, C2], F32, kind="ExternalInput").ap()
    ctx["bo_col_in"] = nc.dram_tensor("bo_col", [128, C2], F32, kind="ExternalInput").ap()
    ctx["bk_rep"] = nc.dram_tensor("bk_rep", [128, D], F32, kind="ExternalInput").ap()
    ctx["bv_rep"] = nc.dram_tensor("bv_rep", [128, D], F32, kind="ExternalInput").ap()
    ctx["pos_col"] = nc.dram_tensor("pos_col", [128, C2], F32, kind="ExternalInput").ap()
    ctx["pos_row"] = nc.dram_tensor("pos_row", [2, T], F32, kind="ExternalInput").ap()
    ctx["h128_in"] = nc.dram_tensor("h128_in", [128, 128], F32, kind="ExternalInput").ap()

    ctx["ternu"] = nc.dram_tensor("ternu", [C4, 128, C2 * 128], BF16, kind="Internal").ap()
    ctx["ternd"] = nc.dram_tensor("ternd", [C2, 128, C4 * 128], BF16, kind="Internal").ap()
    if "attn" in parts:
        for nm, shp in (("wk_pair", [2, 8, 128, C2 * 256]),
                        ("wv_pair", [2, 8, 128, C2 * 256]),
                        ("wq_pair", [2, 8, 128, C2 * 256]),
                        ("wo_pair", [2, C2, 128, C2 * 128]),
                        ("xln_h", [128, C2 * T]), ("xln_l", [128, C2 * T]),
                        ("oj_h", [128, C2 * T]), ("oj_l", [128, C2 * T])):
            ctx[nm] = nc.dram_tensor(nm, shp, BF16, kind="Internal").ap()
        ctx["cc_in"] = nc.dram_tensor("cc_in", [128, H * 129], F32, kind="Internal").ap()
        ctx["cc_out"] = nc.dram_tensor("cc_out", [128, H * 129], F32, kind="Internal").ap()

    with (
        nc.sbuf_tensor("xres", [128, C2 * T], F32) as xres,
        nc.sbuf_tensor("h128t", [128, 128], BF16) as h128,
        nc.sbuf_tensor("consts", [128, 64], F32) as consts,
        nc.sbuf_tensor("smalls", [128, 1344], F32) as smalls,
        nc.Block() as block,
        _sems(nc) as S,
    ):
        pool_cms = [nc.semaphore(f"dmap{i}") for i in range(32)]
        pool = [Sem(cm.__enter__(), f"dmap{i}") for i, cm in enumerate(pool_cms)]
        psum_g = [nc.psum_tensor(f"ps{i}", [128, 512], F32) for i in range(8)]
        psum = [g.__enter__() for g in psum_g]
        try:
            p = P(nc, block)
            p.init_dma_pool(pool)
            ctx.update(p=p, S=S, psum=psum, xres=xres, h128=h128,
                       consts=consts, smalls=smalls)
            emit_setup(ctx)
            with _ffn_ws(nc) as ws:
                ctx["ffn"] = ws
                emit_ffn_prep(ctx)
                if "ffn1" in parts:
                    emit_ffn(ctx, "f1")
                p.flush()
            if "attn" in parts:
                emit_attn(ctx)
            if "ffn2" in parts:
                with _ffn_ws(nc) as ws:
                    ctx["ffn"] = ws
                    emit_ffn(ctx, "f2")
                    p.flush()
            emit_output(ctx)
            p.flush()
            p.check()
        finally:
            for g in reversed(psum_g):
                g.__exit__(None, None, None)
            for cm in reversed(pool_cms):
                cm.__exit__(None, None, None)
    nc.finalize()
    return nc


class _sems:
    NAMES = ("dw", "dx", "pe", "dve", "act", "gp", "st", "cc", "bar")

    def __init__(self, nc):
        self.nc = nc
        self.cms = []

    def __enter__(self):
        class Bag:
            pass
        bag = Bag()
        for nm in self.NAMES:
            cm = self.nc.semaphore(nm)
            setattr(bag, nm, Sem(cm.__enter__(), nm))
            self.cms.append(cm)
        return bag

    def __exit__(self, *a):
        for cm in reversed(self.cms):
            cm.__exit__(*a)
        return False


_WS_ID = [0]


class _ffn_ws:
    def __init__(self, nc):
        self.nc = nc
        self.cms = []

    def __enter__(self):
        class WS:
            pass
        ws = WS()
        _WS_ID[0] += 1
        i = _WS_ID[0]
        for nm, shape, dt in (
                ("wkW", [128, 32 * SB], F32),
                ("q16", [128, C4 * SB], BF16),
                ("qi8", [128, 8 * SB], I8),
                ("cg16", [128, 8 * SB], BF16),
                ("bct", [128, 8 * SB], F32),
                ("wslab", [128, 6144], BF16),
        ):
            cm = self.nc.sbuf_tensor(f"{nm}_{i}", shape, dt)
            setattr(ws, nm, cm.__enter__())
            self.cms.append(cm)
        return ws

    def __exit__(self, *a):
        for cm in reversed(self.cms):
            cm.__exit__(*a)
        return False



def emit_setup(ctx):
    p, S = ctx["p"], ctx["S"]
    xres, consts, smalls = ctx["xres"], ctx["consts"], ctx["smalls"]
    xt = []
    for i in range(4):
        sl = slice(i * (C2 * T // 4), (i + 1) * (C2 * T // 4))
        xt.append(p.dma("sync", xres[:, sl], ctx["x_in"][:, sl]))
    ctx["tok_x_list"] = xt
    ctx["tok_x"] = None
    th = p.dma("sync", smalls[:, 0:128], ctx["h128_in"])
    ctx["tok_h128"] = p.emit(
        "vector", lambda e: e.tensor_copy(ctx["h128"][:], smalls[:, 0:128]),
        waits=[th], inc=S.dve)

    tp_ = p.dma("sync", consts[:, 0:C2], ctx["pos_col"])
    ctx["tok_decay_col"] = p.emit(
        "scalar", lambda e: e.activation(out=consts[:, 0:C2],
                                         in_=consts[:, 0:C2],
                                         func=ACT.Exp, scale=-1.0 / NSEQ),
        waits=[tp_], inc=S.act)
    ctx["decay_col"] = consts[:, 0:C2]
    tb1 = p.dma("sync", consts[:, 16:32], ctx["bq_col_in"])
    tb2 = p.dma("sync", consts[:, 32:48], ctx["bo_col_in"])
    ctx["bq_col"] = consts[:, 16:32]
    ctx["bo_col"] = consts[:, 32:48]
    ctx["tok_bias_list"] = [tb1, tb2]



def emit_ffn_prep(ctx):
    p, S = ctx["p"], ctx["S"]
    ws = ctx["ffn"]
    wk = ws.wkW
    consts = ctx["consts"]

    prev_red = None
    for wi, (name, wt, nk, ncols) in enumerate(
            (("wu", ctx["wup_t"], C2, 4096), ("wd", ctx["wdn_t"], C4, D))):
        nj = ncols // 2048
        nt = nk * nj
        red = ctx["smalls"][:, 128:128 + nt]
        toks = []
        red_toks = []
        for it in range(nt):
            k, j = divmod(it, nj)
            buf = wk[:, (it % 2) * 2048:(it % 2) * 2048 + 2048]
            w_dep = [] if it < 2 else [red_toks[it - 2]]
            if it < 2 and prev_red is not None:
                w_dep = [prev_red]
            tl = p.dma("sync", buf,
                       wt[k * 128:(k + 1) * 128, j * 2048:(j + 1) * 2048],
                       waits=w_dep, spread=True)
            tr = p.emit("vector", lambda e, b=buf, r=red[:, it:it + 1]:
                        e.tensor_reduce(r, b, AXX, ALU.add,
                                        apply_absolute_value=True),
                        waits=[tl], inc=S.dve)
            toks.append(tl)
            red_toks.append(tr)
        tot = consts[:, 56 + wi:57 + wi]
        tf = p.emit("vector", lambda e, r=red, t=tot: e.tensor_reduce(
            t, r, AXX, ALU.add), waits=[red_toks[-1]], inc=S.dve)
        prev_red = tf
        tpar = p.emit("gpsimd", lambda e, t=tot: e.partition_all_reduce(
            t, t, 128, ReduceOp.add), waits=[tf], inc=S.gp)
        cnt = float(2048 * 4096)
        sc = consts[:, 48 + wi * 2:49 + wi * 2]
        th = consts[:, 49 + wi * 2:50 + wi * 2]
        nth = consts[:, 54 + wi:55 + wi]
        t1 = p.emit("vector", lambda e, t=tot, s=sc: e.tensor_scalar(
            out=s, in0=t, scalar1=1.0 / cnt, scalar2=EPS_Q,
            op0=ALU.mult, op1=ALU.max), waits=[tpar], inc=S.dve)
        t2 = p.emit("vector", lambda e, s=sc, t=th: e.tensor_scalar(
            out=t, in0=s, scalar1=0.5, scalar2=None, op0=ALU.mult),
            waits=[t1], inc=S.dve)
        t3 = p.emit("vector", lambda e, s=sc, n=nth: e.tensor_scalar(
            out=n, in0=s, scalar1=-0.5, scalar2=None, op0=ALU.mult),
            waits=[t1], inc=S.dve)
        ctx[f"{name}_scale"] = sc
        ctx[f"{name}_thr"] = (th, nth, t3 if t3[1] > t2[1] else t2)
        ctx[f"{name}_thr_toks"] = (t2, t3)

    for name, wt, tern_d, nk, ncols in (
            ("wu", ctx["wup_t"], ctx["ternu"], C2, 4096),
            ("wd", ctx["wdn_t"], ctx["ternd"], C4, D)):
        th, nth, _ = ctx[f"{name}_thr"]
        t2, t3 = ctx[f"{name}_thr_toks"]
        nj = ncols // 2048
        nslab = ncols // 128
        toks = list(ctx.get("tern_carry_dma", []))
        carry_tt = ctx.get("tern_carry_tt")
        ncar = len(toks)
        for it0 in range(nk * nj):
            k, j = divmod(it0, nj)
            sl = it0 % 2
            buf = wk[:, 4096 + sl * 2048:4096 + sl * 2048 + 2048]
            lbuf = wk[:, sl * 2048:sl * 2048 + 2048]
            tbuf = ws.q16[:, sl * 2048:sl * 2048 + 2048]
            it2 = it0 + ncar
            w_dep = ([t2, t3, prev_red, carry_tt] if it2 < 2 else
                     [toks[it2 - 2]])
            tl = p.dma("sync", buf,
                       wt[k * 128:(k + 1) * 128, j * 2048:(j + 1) * 2048],
                       waits=w_dep, spread=True)
            tlt = p.emit("vector", lambda e, b=buf, l=lbuf, n=nth:
                         e.tensor_scalar(out=l, in0=b, scalar1=n[:, 0:1],
                                         scalar2=None, op0=ALU.is_lt),
                         waits=[tl, t3] + ([toks[it2 - 2]] if it2 >= 2 else []),
                         inc=S.dve)
            tt = p.emit("vector", lambda e, b=buf, l=lbuf, t=tbuf, h=th:
                        e.scalar_tensor_tensor(out=t, in0=b, scalar=h[:, 0:1],
                                               in1=l, op0=ALU.is_gt,
                                               op1=ALU.subtract),
                        waits=[tlt, t2], inc=S.dve)
            nsl = 2048 // 128
            src_ = tbuf.rearrange("p (j c) -> p j c", j=nsl)
            dst = tern_d[j * nsl:(j + 1) * nsl, :, k * 128:(k + 1) * 128] \
                .transpose([1, 0, 2])
            tw = p.dma("sync", dst, src_, waits=[tt], spread=True)
            toks.append(tw)
        ctx[f"tok_tern_{name}"] = toks[-1]
        ctx["tern_carry_dma"] = toks[-2:]
        ctx["tern_carry_tt"] = tt
    ctx["tok_prep_q16"] = None
    ctx["tok_prep_q16_l"] = list(ctx["tern_carry_dma"])




def _tap(*a, **k):
    return


def _wsl(ws, i):
    return ws.wkW[:, i * SB:i * SB + SB]


def emit_ffn(ctx, tag):
    ctx["ffn_slots"] = SlotTracker(32)
    st_prev = None
    for s in range(NSTRIPE):
        st = _ffn_stageA(ctx, s, tag)
        if st_prev is not None:
            _ffn_down(ctx, st_prev, tag)
        _ffn_quant1(ctx, st, tag)
        _ffn_upmid(ctx, st, tag)
        st_prev = st
    _ffn_down(ctx, st_prev, tag)
    ctx["tok_resid_done"] = ctx["tok_down_last"]
    if tag == "f2":
        ctx["tok_final"] = ctx["tok_resid_done"]


def _ffn_stageA(ctx, s, tag):
    p, S = ctx["p"], ctx["S"]
    ws = ctx["ffn"]
    xres = ctx["xres"]
    psum = ctx["psum"]
    h128 = ctx["h128"]
    pf = ctx["psum_free"]
    t0 = s * SB
    wk = ws.wkW
    slots = ctx["ffn_slots"]
    st = {"s": s, "t0": t0}

    def xs(c):
        return xres[:, c * T + t0: c * T + t0 + SB]
    st["xs"] = xs
    x_stripe = xres[:].rearrange("p (c t) -> p c t", c=C2)[:, :, t0:t0 + SB]

    if tag == "f1":
        dep_x = list(ctx["tok_x_list"])
        if s == 0:
            dep_x.extend(ctx.get("tok_prep_q16_l", []))
    else:
        dep_x = [ctx["tok_attn_done"]]
    st["dep_x"] = dep_x

    sq_out = wk[:, 16 * SB:32 * SB].rearrange("p (c t) -> p c t", c=C2)
    t_sq = p.emit("scalar", lambda e: e.activation(
        out=sq_out, in_=x_stripe, func=ACT.Square),
        waits=dep_x + slots.write_waits(range(16, 32)), inc=S.act)
    slots.note_write(range(16, 32), t_sq)

    def fold_region(src_base, n, tok):
        cur = n
        while cur > 1:
            hh = cur // 2
            a = wk[:, src_base * SB:(src_base + hh) * SB] \
                .rearrange("p (c t) -> p c t", c=hh)
            b = wk[:, (src_base + hh) * SB:(src_base + cur) * SB] \
                .rearrange("p (c t) -> p c t", c=hh)
            tok = p.emit("vector", lambda e, a=a, b=b:
                         e.tensor_tensor(a, a, b, ALU.add),
                         waits=[tok], inc=S.dve)
            cur = hh
        return tok

    a0 = wk[:, 0:8 * SB].rearrange("p (c t) -> p c t", c=8)
    t_fx = p.emit("vector", lambda e: e.tensor_tensor(
        a0, x_stripe[:, 0:8, :], x_stripe[:, 8:16, :], ALU.add),
        waits=dep_x + slots.write_waits(range(0, 8)), inc=S.dve)
    slots.note_write(range(0, 8), t_fx)
    t_sx = fold_region(0, 8, t_fx)
    slots.note_write([0], t_sx)
    slots.note_read(range(0, 8), t_sx)
    t_sxx = fold_region(16, 16, t_sq)
    slots.note_write([16], t_sxx)
    slots.note_read(range(16, 32), t_sxx)

    bct = ws.bct
    b_sx = bct[:, 0 * SB:1 * SB]
    b_sxx = bct[:, 1 * SB:2 * SB]
    t1 = p.emit("gpsimd", lambda e: e.partition_all_reduce(
        b_sx, wk[:, 0:SB], 128, ReduceOp.add), waits=[t_sx], inc=S.gp)
    slots.note_read([0], t1)
    t2 = p.emit("gpsimd", lambda e: e.partition_all_reduce(
        b_sxx, wk[:, 16 * SB:17 * SB], 128, ReduceOp.add),
        waits=[t_sxx], inc=S.gp)
    slots.note_read([16], t2)

    b_mu = bct[:, 2 * SB:3 * SB]
    b_r = bct[:, 3 * SB:4 * SB]
    b_t1 = bct[:, 4 * SB:5 * SB]
    b_t2 = bct[:, 5 * SB:6 * SB]
    tm = p.emit("vector", lambda e: e.tensor_scalar(
        out=b_mu, in0=b_sx, scalar1=1.0 / D, scalar2=None, op0=ALU.mult),
        waits=[t1], inc=S.dve)
    tv1 = p.emit("vector", lambda e: e.tensor_scalar(
        out=b_t1, in0=b_sxx, scalar1=1.0 / D, scalar2=None, op0=ALU.mult),
        waits=[t2], inc=S.dve)
    tv2 = p.emit("vector", lambda e: e.tensor_tensor(b_t2, b_mu, b_mu, ALU.mult),
                 waits=[tm], inc=S.dve)
    tv3 = p.emit("vector", lambda e: e.scalar_tensor_tensor(
        out=b_t1, in0=b_t1, scalar=EPS_LN, in1=b_t2,
        op0=ALU.add, op1=ALU.subtract), waits=[tv1, tv2], inc=S.dve)
    ts1 = p.emit("scalar", lambda e: e.activation(
        out=b_t2, in_=b_t1, func=ACT.Sqrt), waits=[tv3], inc=S.act)
    t_r = p.emit("vector", lambda e: e.reciprocal(b_r, b_t2),
                 waits=[ts1], inc=S.dve)

    cg = ws.cg16
    t_h = ctx["tok_h128"]
    ev_toks = []
    for c in range(C2):
        xh = cg[:, (c % 2) * SB:(c % 2) * SB + SB]
        xl = cg[:, (2 + c % 2) * SB:(2 + c % 2) * SB + SB]
        w_prev = [ev_toks[c - 2]] if c >= 2 else []
        th_ = p.emit("scalar", lambda e, o=xh, i=xs(c): e.activation(
            out=o, in_=i, func=ACT.Copy),
            waits=dep_x + w_prev + [ctx.get("tok_prev_q2")], inc=S.act)
        tl_ = p.emit("vector", lambda e, o=xl, i=xs(c), h=xh:
                     e.tensor_tensor(o, i, h, ALU.subtract),
                     waits=[th_], inc=S.dve)
        ps = psum[c % 2]
        mm_dep = [th_, tl_, t_h, pf[c % 2]]
        p.emit("tensor", lambda e, ps=ps, x=xh: e.matmul(
            ps[:, 0:SB], h128[:], x, start=True, stop=False), waits=mm_dep)
        tmm = p.emit("tensor", lambda e, ps=ps, x=xl: e.matmul(
            ps[:, 0:SB], h128[:], x, start=False, stop=True), inc=S.pe)
        dst = c
        tev = p.emit("scalar", lambda e, ps=ps, o=_wsl(ws, dst): e.activation(
            out=o, in_=ps[:, 0:SB], func=ACT.Copy),
            waits=[tmm] + slots.write_waits([dst]), inc=S.act)
        slots.note_write([dst], tev)
        pf[c % 2] = tev
        ev_toks.append(tev)

    tok_b = _butterflies(ctx, slots, C2, ws, eng0=0)
    tz = p.emit("vector", lambda e, a=wk[0:1, 0:SB]:
                e.memset(a, 0.0), waits=[tok_b] + slots.write_waits([0]),
                inc=S.dve)
    slots.note_write([0], tz)

    b_mw = bct[:, 6 * SB:7 * SB]
    t_mw = _absmax(ctx, slots, C2, b_mw, tz)
    b_F = bct[:, 2 * SB:3 * SB]
    b_A1 = bct[:, 0 * SB:1 * SB]
    tq1 = p.emit("vector", lambda e: e.tensor_tensor(b_t1, b_r, b_mw, ALU.mult),
                 waits=[t_mw, t_r], inc=S.dve)
    tq2 = p.emit("vector", lambda e: e.tensor_scalar(
        out=b_t1, in0=b_t1, scalar1=EPS_Q * RT2048, scalar2=None, op0=ALU.max),
        waits=[tq1], inc=S.dve)
    tq3 = p.emit("vector", lambda e: e.reciprocal(b_t2, b_t1),
                 waits=[tq2], inc=S.dve)
    st["t_F"] = p.emit("vector", lambda e: e.scalar_tensor_tensor(
        out=b_F, in0=b_t2, scalar=QB, in1=b_r, op0=ALU.mult, op1=ALU.mult),
        waits=[tq3], inc=S.dve)
    st["t_A1"] = p.emit("vector", lambda e: e.tensor_scalar(
        out=b_A1, in0=b_t1, scalar1=ctx["wu_scale"][:, 0:1],
        scalar2=1.0 / (QB * RT2048), op0=ALU.mult, op1=ALU.mult),
        waits=[tq2], inc=S.dve)
    st["b_F"] = b_F
    st["b_A1"] = b_A1
    st["b_t1"] = b_t1
    st["b_t2"] = b_t2
    return st


def _ffn_quant1(ctx, st, tag):
    slots = ctx["ffn_slots"]
    st["t_q1"] = _quant(ctx, slots, st["b_F"], st["t_F"], n=C2,
                       extra=[t for t in ctx.get("tok_down_q16", []) if t])


def _ffn_upmid(ctx, st, tag):
    p, S = ctx["p"], ctx["S"]
    ws = ctx["ffn"]
    psum = ctx["psum"]
    h128 = ctx["h128"]
    smalls = ctx["smalls"]
    pf = ctx["psum_free"]
    cg = ws.cg16
    wk = ws.wkW
    slots = ctx["ffn_slots"]
    bct = ws.bct
    t_q1 = st["t_q1"]
    t_h = ctx["tok_h128"]
    b_t1, b_t2, b_A1 = st["b_t1"], st["b_t2"], st["b_A1"]

    wslab = ws.wslab
    up_dep = [ctx["tok_tern_wu"], t_q1]
    half_used = ctx.setdefault("up_half_state", [[None, None], [None, None]])
    trtok = ctx.setdefault("up_trtok", [None, None])
    ggtok = ctx.setdefault("up_ggtok", [None, None])
    ghtok = ctx.setdefault("up_ghtok", [None, None])
    for m in range(C4):
        ps = psum[m % 2]
        ps2 = psum[2 + m % 2]
        tmm = None
        for kh in range(2):
            hb = wslab[:, kh * 1024:kh * 1024 + 1024]
            t_sl = p.dma("sync", hb,
                         ctx["ternu"][m][:, kh * 1024:(kh + 1) * 1024],
                         waits=up_dep + [t for t in half_used[kh] if t],
                         spread=True)
            half_used[kh] = [t_sl, half_used[kh][1]]
            for kk in range(8):
                k = kh * 8 + kk
                q_c = ws.q16[:, k * SB:(k + 1) * SB]
                extra = [t_sl, t_q1] + ([pf[m % 2]] if k == 0 else [])
                tmm = p.emit("tensor", lambda e, ps=ps,
                             w=hb[:, kk * 128:(kk + 1) * 128], q=q_c,
                             st=(k == 0), sp=(k == C2 - 1): e.matmul(
                    ps[:, 0:SB], w, q, start=st, stop=sp),
                    waits=extra if kk == 0 else [],
                    inc=S.pe if kk == 7 else None)
            half_used[kh] = [half_used[kh][0], tmm]
        tr_ = smalls[:, (m % 2) * SB:(m % 2) * SB + SB]
        gg = smalls[:, 512 + (m % 2) * SB:512 + (m % 2) * SB + SB]
        gh = cg[:, (m % 2) * SB:(m % 2) * SB + SB]
        gl = cg[:, (2 + m % 2) * SB:(2 + m % 2) * SB + SB]
        ta = p.emit("scalar", lambda e, ps=ps, o=tr_: e.activation(
            out=o, in_=ps[:, 0:SB], func=ACT.Relu),
            waits=[tmm, trtok[m % 2]], inc=S.act)
        pf[m % 2] = ta
        tb = p.emit("vector", lambda e, o=gg, t=tr_: e.tensor_tensor(
            o, t, t, ALU.mult), waits=[ta, ggtok[m % 2]], inc=S.dve)
        trtok[m % 2] = tb
        tc = p.emit("scalar", lambda e, o=gh, g=gg: e.activation(
            out=o, in_=g, func=ACT.Copy), waits=[tb, ghtok[m % 2]], inc=S.act)
        td = p.emit("vector", lambda e, o=gl, g=gg, h=gh: e.tensor_tensor(
            o, g, h, ALU.subtract), waits=[tc], inc=S.dve)
        ggtok[m % 2] = td
        mm2_dep = [tc, td, t_h, pf[2 + m % 2]]
        p.emit("tensor", lambda e, ps=ps2, g=gh: e.matmul(
            ps[:, 0:SB], h128[:], g, start=True, stop=False), waits=mm2_dep)
        tm2 = p.emit("tensor", lambda e, ps=ps2, g=gl: e.matmul(
            ps[:, 0:SB], h128[:], g, start=False, stop=True), inc=S.pe)
        dst = m
        tev = p.emit("scalar", lambda e, ps=ps2, o=_wsl(ws, dst): e.activation(
            out=o, in_=ps[:, 0:SB], func=ACT.Copy),
            waits=[tm2] + slots.write_waits([dst]), inc=S.act)
        slots.note_write([dst], tev)
        pf[2 + m % 2] = tev
        ghtok[m % 2] = tm2

    tok_b4 = _butterflies(ctx, slots, C4, ws, eng0=1)
    b_mg = bct[:, 6 * SB:7 * SB]
    t_mg = _absmax(ctx, slots, C4, b_mg, tok_b4)
    b_G = bct[:, 7 * SB:8 * SB]
    tg1 = p.emit("vector", lambda e: e.tensor_tensor(b_t2, b_A1, b_A1, ALU.mult),
                 waits=[st["t_A1"], t_mg], inc=S.dve)
    tg2 = p.emit("vector", lambda e: e.tensor_tensor(b_t1, b_t2, b_mg, ALU.mult),
                 waits=[tg1], inc=S.dve)
    tg3 = p.emit("vector", lambda e: e.tensor_scalar(
        out=b_t1, in0=b_t1, scalar1=EPS_Q * RT4096, scalar2=None, op0=ALU.max),
        waits=[tg2], inc=S.dve)
    tg4 = p.emit("vector", lambda e: e.reciprocal(b_G, b_t1),
                 waits=[tg3], inc=S.dve)
    t_G = p.emit("vector", lambda e: e.scalar_tensor_tensor(
        out=b_G, in0=b_G, scalar=QB, in1=b_t2, op0=ALU.mult, op1=ALU.mult),
        waits=[tg4], inc=S.dve)
    b_A2h = ctx["smalls"][:, 1024:1280]
    st["t_A2h"] = p.emit("vector", lambda e: e.tensor_scalar(
        out=b_A2h, in0=b_t1, scalar1=ctx["wd_scale"][:, 0:1],
        scalar2=0.5 / (QB * RT4096), op0=ALU.mult, op1=ALU.mult),
        waits=[tg3, ctx.get("tok_down_last")], inc=S.dve)
    st["b_A2h"] = b_A2h
    st["t_q2"] = _quant(ctx, slots, b_G, t_G, n=C4)
    ctx["tok_prev_q2"] = st["t_q2"]


def _ffn_down(ctx, st, tag):
    p, S = ctx["p"], ctx["S"]
    ws = ctx["ffn"]
    psum = ctx["psum"]
    smalls = ctx["smalls"]
    pf = ctx["psum_free"]
    wslab = ws.wslab
    t_q2 = st["t_q2"]
    xs = st["xs"]
    b_A2h = st["b_A2h"]
    dn_dep = [ctx["tok_tern_wd"], t_q2]
    half_used = ctx.setdefault("dn_half_state", [[None, None], [None, None]])
    t_res = None
    tbs = [None, None]
    for m in range(C2):
        ps = psum[4 + m % 2]
        tmm = None
        for kh in range(2):
            hb = wslab[:, 2048 + kh * 2048:2048 + kh * 2048 + 2048]
            t_sl = p.dma("sync", hb,
                         ctx["ternd"][m][:, kh * 2048:(kh + 1) * 2048],
                         waits=dn_dep + [t for t in half_used[kh] if t],
                         spread=True)
            half_used[kh] = [t_sl, half_used[kh][1]]
            for kk in range(C2):
                k = kh * C2 + kk
                q_c = ws.q16[:, k * SB:(k + 1) * SB]
                extra = [t_sl, t_q2] + ([pf[4 + m % 2]] if k == 0 else [])
                tmm = p.emit("tensor", lambda e, ps=ps,
                             w=hb[:, kk * 128:(kk + 1) * 128], q=q_c,
                             st=(k == 0), sp=(k == C4 - 1): e.matmul(
                    ps[:, 0:SB], w, q, start=st, stop=sp),
                    waits=extra if kk == 0 else [],
                    inc=S.pe if kk == C2 - 1 else None)
            half_used[kh] = [half_used[kh][0], tmm]
        tt_ = smalls[:, (m % 2) * SB:(m % 2) * SB + SB]
        ta = p.emit("vector", lambda e, ps=ps, o=tt_: e.tensor_tensor(
            o, ps[:, 0:SB], b_A2h, ALU.mult),
            waits=[tmm, st["t_A2h"]], inc=S.dve)
        pf[4 + m % 2] = ta
        tb = p.emit("vector", lambda e, x=xs(m), t=tt_: e.tensor_tensor(
            x, x, t, ALU.add), waits=[ta], inc=S.dve)
        tbs[m % 2] = tb
        t_res = tb
    ctx["tok_down_last"] = t_res
    ctx["tok_down_q16"] = [half_used[0][1], half_used[1][1]]
    tr = ctx.setdefault("up_trtok", [None, None])
    tr[0] = tbs[0] or tr[0]
    tr[1] = tbs[1] or tr[1]


def _butterflies(ctx, slots, n, ws, eng0):
    p, S = ctx["p"], ctx["S"]
    wk = ws.wkW
    nstages = n.bit_length() - 1
    tok = None
    for st in range(nstages):
        hh = 1 << st
        g = n // (2 * hh)
        full = wk[:, 0:n * SB].rearrange("p (g two h t) -> p g two h t",
                                         g=g, two=2, h=hh)
        a_set = full[:, :, 0, :, :]
        b_set = full[:, :, 1, :, :]
        eng = ("vector", "gpsimd")[(eng0 + st) % 2]
        sem = S.dve if eng == "vector" else S.gp
        w_dep = ([tok] if tok else []) + slots.write_waits(range(n))
        t1 = p.emit(eng, lambda e, a=a_set, b=b_set:
                    e.tensor_tensor(b, a, b, ALU.subtract),
                    waits=w_dep, inc=sem)
        tok = p.emit("vector", lambda e, a=a_set, b=b_set:
                     e.scalar_tensor_tensor(out=a, in0=a, scalar=2.0, in1=b,
                                            op0=ALU.mult, op1=ALU.subtract),
                     waits=[t1], inc=S.dve)
        slots.note_write(range(n), tok)
    return tok


def _absmax(ctx, slots, n, out_bc, tok_b):
    p, S = ctx["p"], ctx["S"]
    ws = ctx["ffn"]
    wk = ws.wkW
    in3 = wk[:, 0:n * SB].rearrange("p (c t) -> p t c", c=n)
    out3 = out_bc.rearrange("p (t o) -> p t o", o=1)
    t = p.emit("vector", lambda e: e.tensor_reduce(
        out3, in3, AXX, ALU.max, apply_absolute_value=True),
        waits=[tok_b] + slots.write_waits(range(n)), inc=S.dve)
    slots.note_read(range(n), t)
    tp = p.emit("gpsimd", lambda e: e.partition_all_reduce(
        out_bc, out_bc, 128, ReduceOp.absmax), waits=[t], inc=S.gp)
    return tp


def _quant(ctx, slots, b_F, t_F, n, extra=()):
    p, S = ctx["p"], ctx["S"]
    ws = ctx["ffn"]
    wk = ws.wkW
    cg = ws.cg16
    last = None
    prev_q = None
    prev_t2 = None
    for g in range(n // 8):
        b0 = g * 8
        grp = wk[:, b0 * SB:(b0 + 8) * SB]
        grp3 = grp.rearrange("p (c t) -> p c t", c=8)
        fb = b_F.rearrange("p (o t) -> p o t", o=1).broadcast_to([128, 8, SB])
        tu = p.emit("vector", lambda e, g3=grp3, f=fb: e.tensor_tensor(
            g3, g3, f, ALU.mult),
            waits=[t_F] + ctx.get("tok_prep_q16_l", []) + list(extra) +
                  slots.write_waits(range(b0, b0 + 8)), inc=S.dve)
        slots.note_write(range(b0, b0 + 8), tu)
        qig = ws.qi8[:, 0:8 * SB]
        tqi = p.emit("vector", lambda e, q=qig, u=grp: e.tensor_copy(q, u),
                     waits=[tu] + ([prev_q] if prev_q else []), inc=S.dve)
        t2s = []
        for sub in range(2):
            u4 = wk[:, (b0 + sub * 4) * SB:(b0 + sub * 4 + 4) * SB]
            q4 = ws.qi8[:, sub * 4 * SB:(sub + 1) * 4 * SB]
            ch = cg[:, 0:4 * SB]
            cge = cg[:, 4 * SB:8 * SB]
            gp_dep = [tqi] + ([prev_t2] if prev_t2 else [])
            tch = p.emit("vector", lambda e, o=ch, u=u4, q=q4:
                         e.scalar_tensor_tensor(out=o, in0=u, scalar=-0.5,
                                                in1=q, op0=ALU.add,
                                                op1=ALU.is_gt),
                         waits=gp_dep, inc=S.dve)
            tcg = p.emit("vector", lambda e, o=cge, u=u4, q=q4:
                         e.scalar_tensor_tensor(out=o, in0=u, scalar=0.5,
                                                in1=q, op0=ALU.add,
                                                op1=ALU.is_ge),
                         waits=[tch], inc=S.dve)
            t2 = p.emit("gpsimd", lambda e, o=u4, a=ch, b=cge:
                        e.tensor_tensor(o, a, b, ALU.add),
                        waits=[tcg], inc=S.gp)
            prev_t2 = t2
            t2s.append(t2)
        qo = ws.q16[:, b0 * SB:(b0 + 8) * SB]
        tq = p.emit("vector", lambda e, o=qo, t=grp, q=qig:
                    e.scalar_tensor_tensor(out=o, in0=t, scalar=-1.0, in1=q,
                                           op0=ALU.add, op1=ALU.add),
                    waits=t2s, inc=S.dve)
        slots.note_read(range(b0, b0 + 8), tq)
        slots.note_write(range(b0, b0 + 8), tq)
        prev_q = tq
        last = tq
    return last


class _scope:
    def __init__(self, nc, spec):
        self.nc = nc
        self.spec = spec
        self.cms = []

    def __enter__(self):
        class WS:
            pass
        ws = WS()
        _WS_ID[0] += 1
        i = _WS_ID[0]
        for nm, shape, dt in self.spec:
            cm = self.nc.sbuf_tensor(f"{nm}_{i}", shape, dt)
            setattr(ws, nm, cm.__enter__())
            self.cms.append(cm)
        return ws

    def __exit__(self, *a):
        for cm in reversed(self.cms):
            cm.__exit__(*a)
        return False


def emit_attn(ctx):
    nc = ctx["nc"]
    p = ctx["p"]
    ctx["ones_col"] = nc.const_aps.tensor(1.0, [1, 128])
    with _scope(nc, (
            ("kvacc", [128, 8 * 258], F32),
            ("grow", [2, T], F32),
    )) as aperm:
        ctx["aperm"] = aperm
        _emit_decay_rows(ctx)
        with _scope(nc, (
                ("awk", [128, 26 * SB], F32),
                ("abct", [128, 8 * SB], F32),
                ("acg", [128, 4 * SB], BF16),
        )) as aws:
            ctx["aws"] = aws
            _emit_ln2_spill(ctx)
            p.flush()
        with _scope(nc, (
                ("sbuf32", [128, 2 * 512], F32),
                ("sb16", [128, 4 * 512], BF16),
        )) as ss:
            _emit_wsplit(ctx, ss)
            p.flush()
        with _scope(nc, (
                ("wslabs", [128, 4 * C2 * 256], BF16),
                ("xt", [128, 2 * 2 * 2048], BF16),
                ("ksub", [128, 256], F32),
                ("vsub", [128, 258], F32),
                ("tsc", [128, 256], F32),
                ("bks", [128, 2 * 256], F32),
        )) as ws1:
            ctx["ws1"] = ws1
            _emit_phase1(ctx)
            p.flush()
        _emit_allreduce(ctx)
        with _scope(nc, (
                ("wqs", [128, 2 * C2 * 256], BF16),
                ("xc2", [128, 4 * 512], BF16),
                ("qbuf", [128, 2 * 512], F32),
                ("obuf", [128, 2 * 512], F32),
                ("opr", [128, 4 * 512], BF16),
                ("srow2", [2, 2 * 512], F32),
        )) as ws2:
            ctx["ws2"] = ws2
            _emit_phase2(ctx)
            p.flush()
        with _scope(nc, (
                ("oblk", [128, 2 * C2 * 512], BF16),
                ("wos", [128, 2 * 2 * 2048], BF16),
        )) as ws3:
            ctx["ws3"] = ws3
            _emit_wo(ctx)
            p.flush()


def _emit_decay_rows(ctx):
    p, S = ctx["p"], ctx["S"]
    ap = ctx["aperm"]
    t = p.dma("sync", ap.grow[:, :], ctx["pos_row"])
    ctx["tok_decay_rows"] = p.emit(
        "scalar", lambda e: e.activation(out=ap.grow[:, :], in_=ap.grow[:, :],
                                         func=ACT.Exp, scale=-1.0 / NSEQ),
        waits=[t], inc=S.act)


def _emit_ln2_spill(ctx):
    p, S = ctx["p"], ctx["S"]
    aws = ctx["aws"]
    xres = ctx["xres"]
    wk = aws.awk
    bct = aws.abct
    acg = aws.acg
    dep0 = [ctx["tok_resid_done"]]
    slots = SlotTracker(26)
    spill_toks = []
    prev = [[], []]
    for s in range(NSTRIPE):
        t0 = s * SB
        x_stripe = xres[:].rearrange("p (c t) -> p c t", c=C2)[:, :, t0:t0 + SB]

        sq_out = wk[:, 8 * SB:24 * SB].rearrange("p (c t) -> p c t", c=C2)
        t_sq = p.emit("scalar", lambda e, o=sq_out, i=x_stripe: e.activation(
            out=o, in_=i, func=ACT.Square),
            waits=dep0 + slots.write_waits(range(8, 24)), inc=S.act)
        slots.note_write(range(8, 24), t_sq)

        a0 = wk[:, 0:8 * SB].rearrange("p (c t) -> p c t", c=8)
        t_fx = p.emit("vector", lambda e, o=a0, a=x_stripe[:, 0:8, :],
                      b=x_stripe[:, 8:16, :]: e.tensor_tensor(o, a, b, ALU.add),
                      waits=dep0 + slots.write_waits(range(0, 8)), inc=S.dve)
        slots.note_write(range(0, 8), t_fx)

        def fold_region(base, n, tok):
            cur = n
            while cur > 1:
                hh = cur // 2
                a = wk[:, base * SB:(base + hh) * SB].rearrange(
                    "p (c t) -> p c t", c=hh)
                b = wk[:, (base + hh) * SB:(base + cur) * SB].rearrange(
                    "p (c t) -> p c t", c=hh)
                tok = p.emit("vector", lambda e, a=a, b=b:
                             e.tensor_tensor(a, a, b, ALU.add),
                             waits=[tok], inc=S.dve)
                cur = hh
            return tok

        t_sx = fold_region(0, 8, t_fx)
        slots.note_write([0], t_sx)
        slots.note_read(range(0, 8), t_sx)
        t_sxx = fold_region(8, 16, t_sq)
        slots.note_write([8], t_sxx)
        slots.note_read(range(8, 24), t_sxx)

        b_sx = bct[:, 0:SB]
        b_sxx = bct[:, SB:2 * SB]
        t1 = p.emit("gpsimd", lambda e: e.partition_all_reduce(
            b_sx, wk[:, 0:SB], 128, ReduceOp.add), waits=[t_sx], inc=S.gp)
        slots.note_read([0], t1)
        t2 = p.emit("gpsimd", lambda e: e.partition_all_reduce(
            b_sxx, wk[:, 8 * SB:9 * SB], 128, ReduceOp.add),
            waits=[t_sxx], inc=S.gp)
        slots.note_read([8], t2)

        b_mu = bct[:, 2 * SB:3 * SB]
        b_r = bct[:, 3 * SB:4 * SB]
        b_t1 = bct[:, 4 * SB:5 * SB]
        b_t2 = bct[:, 5 * SB:6 * SB]
        b_mur = bct[:, 6 * SB:7 * SB]
        tm = p.emit("vector", lambda e: e.tensor_scalar(
            out=b_mu, in0=b_sx, scalar1=1.0 / D, scalar2=None, op0=ALU.mult),
            waits=[t1], inc=S.dve)
        tv1 = p.emit("vector", lambda e: e.tensor_scalar(
            out=b_t1, in0=b_sxx, scalar1=1.0 / D, scalar2=None, op0=ALU.mult),
            waits=[t2], inc=S.dve)
        tv2 = p.emit("vector", lambda e: e.tensor_tensor(b_t2, b_mu, b_mu,
                                                         ALU.mult),
                     waits=[tm], inc=S.dve)
        tv3 = p.emit("vector", lambda e: e.scalar_tensor_tensor(
            out=b_t1, in0=b_t1, scalar=EPS_LN, in1=b_t2,
            op0=ALU.add, op1=ALU.subtract), waits=[tv1, tv2], inc=S.dve)
        ts1 = p.emit("scalar", lambda e: e.activation(
            out=b_t2, in_=b_t1, func=ACT.Sqrt), waits=[tv3], inc=S.act)
        t_r = p.emit("vector", lambda e: e.reciprocal(b_r, b_t2),
                     waits=[ts1], inc=S.dve)
        t_mur = p.emit("vector", lambda e: e.tensor_tensor(b_mur, b_mu, b_r,
                                                           ALU.mult),
                       waits=[t_r], inc=S.dve)

        for c in range(C2):
            sl = wk[:, 24 * SB:25 * SB]
            sl2 = wk[:, 25 * SB:26 * SB]
            xh = acg[:, (c % 2) * SB:(c % 2) * SB + SB]
            xl = acg[:, (2 + c % 2) * SB:(2 + c % 2) * SB + SB]
            xc = xres[:, c * T + t0:c * T + t0 + SB]
            w_prev = list(prev[c % 2])
            ta = p.emit("vector", lambda e, o=sl, x=xc: e.tensor_tensor(
                o, x, b_r, ALU.mult),
                waits=[t_r, t_mur] + slots.write_waits([24]), inc=S.dve)
            tb = p.emit("vector", lambda e, o=sl2, a=sl: e.tensor_tensor(
                o, a, b_mur, ALU.subtract),
                waits=[ta] + slots.write_waits([25]), inc=S.dve)
            slots.note_write([24], ta)
            slots.note_write([25], tb)
            th = p.emit("scalar", lambda e, o=xh, i=sl2: e.activation(
                out=o, in_=i, func=ACT.Copy), waits=[tb] + w_prev, inc=S.act)
            tl = p.emit("vector", lambda e, o=xl, i=sl2, h=xh: e.tensor_tensor(
                o, i, h, ALU.subtract), waits=[th], inc=S.dve)
            slots.note_read([25], tl)
            toks = []
            for half, buf, tok_src in ((0, xh, th), (1, xl, tl)):
                dstt = ctx["xln_h" if half == 0 else "xln_l"]
                dst = dstt[:].rearrange("p (ts kc) -> p ts kc", ts=16)[
                    :, 2 * s:2 * s + 2, c * 128:(c + 1) * 128]
                srcb = buf.rearrange("p (two t) -> p two t", two=2)
                toks.append(p.dma("sync", dst, srcb, waits=[tok_src]))
            prev[c % 2] = toks
            spill_toks.extend(toks)
    ctx["tok_xln"] = spill_toks[-4:]


def _emit_wsplit(ctx, ss):
    p, S = ctx["p"], ctx["S"]
    carry = []
    last_lo = None
    for nm, pairnm in (("wk_t", "wk_pair"), ("wv_t", "wv_pair"),
                       ("wq_t", "wq_pair"), ("wo_t", "wo_pair")):
        wt = ctx[nm]
        pair = ctx[pairnm]
        toks = list(carry)
        for it0 in range(C2 * 4):
            it = it0 + len(carry)
            k, j = divmod(it0, 4)
            sl = it0 % 2
            buf = ss.sbuf32[:, sl * 512:sl * 512 + 512]
            hbuf = ss.sb16[:, sl * 512:sl * 512 + 512]
            lbuf = ss.sb16[:, 1024 + sl * 512:1024 + sl * 512 + 512]
            w_dep = ([last_lo] if it < 2 else []) + \
                ([toks[it - 2]] if it >= 2 else [])
            tl_ = p.dma("sync", buf,
                        wt[k * 128:(k + 1) * 128, j * 512:(j + 1) * 512],
                        waits=w_dep)
            th_ = p.emit("scalar", lambda e, b=buf, h=hbuf: e.activation(
                out=h, in_=b, func=ACT.Copy),
                waits=[tl_] + ([toks[it - 2]] if it >= 2 else []), inc=S.act)
            tlo = p.emit("vector", lambda e, b=buf, h=hbuf, l=lbuf:
                         e.tensor_tensor(l, b, h, ALU.subtract),
                         waits=[th_], inc=S.dve)
            tws = []
            for hw, srcb in ((0, hbuf), (1, lbuf)):
                if nm == "wo_t":
                    src2 = srcb.rearrange("p (f c) -> p f c", f=4)
                    dst = pair[hw, j * 4:(j + 1) * 4, :,
                               k * 128:(k + 1) * 128].transpose([1, 0, 2])
                else:
                    src2 = srcb.rearrange("p (s c) -> p s c", s=2)
                    dst = pair[hw, j * 2:(j + 1) * 2, :,
                               k * 256:(k + 1) * 256].transpose([1, 0, 2])
                tws.append(p.dma("sync", dst, src2, waits=[tlo]))
            toks.append(tws[1])
            last_lo = tlo
        ctx[f"tok_{pairnm}"] = toks[-2:]
        carry = toks[-2:]


def _emit_phase1(ctx):
    p, S = ctx["p"], ctx["S"]
    ws1 = ctx["ws1"]
    psum = ctx["psum"]
    aperm = ctx["aperm"]
    pf = ctx["psum_free"]
    dep_w = ctx["tok_wk_pair"] + ctx["tok_wv_pair"]
    dep_x = ctx["tok_xln"]
    decay_col = ctx["decay_col"]
    t_ones = p.emit("vector", lambda e: e.memset(ws1.vsub[:, 128:129], 1.0),
                    inc=S.dve)
    t_ones = p.emit("vector", lambda e: e.memset(ws1.vsub[:, 257:258], 1.0),
                    waits=[t_ones], inc=S.dve)
    kv_ev = []
    xt_state = [[None, None], [None, None]]
    slab_gate = None
    bks_gate = None
    prev_kv = [None, None]
    k_gate = t_ones
    v_gate = t_ones
    for sl in range(8):
        sdep = dep_w + ([slab_gate] if slab_gate else [])
        slab_toks = []
        for qi_, (pairnm, hw) in enumerate((("wk_pair", 0), ("wk_pair", 1),
                                            ("wv_pair", 0), ("wv_pair", 1))):
            dst = ws1.wslabs[:, qi_ * 4096:(qi_ + 1) * 4096]
            slab_toks.append(p.dma("sync", dst, ctx[pairnm][hw, sl],
                                   waits=sdep))
        tbk = p.dma("sync", ws1.bks[:, 0:256],
                    ctx["bk_rep"][:, sl * 256:(sl + 1) * 256],
                    waits=[bks_gate] if bks_gate else [])
        tbv = p.dma("sync", ws1.bks[:, 256:512],
                    ctx["bv_rep"][:, sl * 256:(sl + 1) * 256], waits=[tbk])
        kvb = psum[2 + sl % 2]
        kv_last = None
        mmk = None
        for ts in range(16):
            xb = ts % 2
            xh_t = ws1.xt[:, xb * 4096:xb * 4096 + 2048]
            xl_t = ws1.xt[:, xb * 4096 + 2048:xb * 4096 + 4096]
            xdep = dep_x + [t for t in xt_state[xb] if t]
            t_xh = p.dma("sync", xh_t,
                         ctx["xln_h"][:].rearrange("p (t k) -> p t k",
                                                   t=16)[:, ts, :],
                         waits=xdep)
            t_xl = p.dma("sync", xl_t,
                         ctx["xln_l"][:].rearrange("p (t k) -> p t k",
                                                   t=16)[:, ts, :],
                         waits=[t_xh])
            psK = psum[0]
            psV = psum[1]
            for k in range(C2):
                xh_c = xh_t[:, k * 128:(k + 1) * 128]
                xl_c = xl_t[:, k * 128:(k + 1) * 128]
                wkh = ws1.wslabs[:, 0 * 4096 + k * 256:0 * 4096 + (k + 1) * 256]
                wkl = ws1.wslabs[:, 1 * 4096 + k * 256:1 * 4096 + (k + 1) * 256]
                wvh = ws1.wslabs[:, 2 * 4096 + k * 256:2 * 4096 + (k + 1) * 256]
                wvl = ws1.wslabs[:, 3 * 4096 + k * 256:3 * 4096 + (k + 1) * 256]
                w0 = []
                if k == 0:
                    w0 = [t_xh, t_xl] + slab_toks + \
                        [t for t in (pf[0], pf[1]) if t]
                st = (k == 0)
                sp = (k == C2 - 1)
                p.emit("tensor", lambda e, o=psK, x=xh_c, w=wkh, st=st:
                       e.matmul(o[:, 0:256], x, w, start=st, stop=False),
                       waits=w0)
                p.emit("tensor", lambda e, o=psK, x=xh_c, w=wkl:
                       e.matmul(o[:, 0:256], x, w, start=False, stop=False))
                p.emit("tensor", lambda e, o=psK, x=xl_c, w=wkh, sp=sp:
                       e.matmul(o[:, 0:256], x, w, start=False, stop=sp))
                p.emit("tensor", lambda e, o=psV, x=xh_c, w=wvh, st=st:
                       e.matmul(o[:, 0:256], x, w, start=st, stop=False),
                       waits=w0)
                p.emit("tensor", lambda e, o=psV, x=xh_c, w=wvl:
                       e.matmul(o[:, 0:256], x, w, start=False, stop=False))
                mmk = p.emit("tensor", lambda e, o=psV, x=xl_c, w=wvh, sp=sp:
                             e.matmul(o[:, 0:256], x, w, start=False, stop=sp),
                             inc=S.pe if sp else None)
            xt_state[xb] = [t_xl, mmk]
            tk1 = p.emit("vector", lambda e, o=ws1.tsc[:, 0:256], i=psK:
                         e.scalar_tensor_tensor(
                             out=o, in0=i[:, 0:256], scalar=1.0,
                             in1=ws1.bks[:, 0:256], op0=ALU.mult, op1=ALU.add),
                         waits=[mmk, tbk, k_gate], inc=S.dve)
            pf[0] = tk1
            tk2 = p.emit("vector", lambda e, o=ws1.ksub[:, 0:256],
                         i=ws1.tsc[:, 0:256], dc=decay_col[:, ts:ts + 1]:
                         e.tensor_scalar(out=o, in0=i, scalar1=0.0, scalar2=dc,
                                         op0=ALU.max, op1=ALU.mult),
                         waits=[tk1, ctx["tok_decay_col"], k_gate], inc=S.dve)
            vout = ws1.vsub[:].rearrange("p (h c) -> p h c", h=2)[:, :, 0:128]
            tv = p.emit("vector", lambda e, o=vout, i=psV:
                        e.scalar_tensor_tensor(
                            out=o,
                            in0=i[:, 0:256].rearrange("p (h c) -> p h c", h=2),
                            scalar=1.0,
                            in1=ws1.bks[:, 256:512].rearrange(
                                "p (h c) -> p h c", h=2),
                            op0=ALU.mult, op1=ALU.add),
                        waits=[mmk, tbv, v_gate], inc=S.dve)
            pf[1] = tv
            for h2 in range(2):
                kv_last = p.emit(
                    "tensor", lambda e, o=kvb,
                    l=ws1.ksub[:, h2 * 128:(h2 + 1) * 128],
                    r=ws1.vsub[:, h2 * 129:h2 * 129 + 129],
                    st=(ts == 0 and h2 == 0), sp=(ts == 15 and h2 == 1),
                    h2=h2:
                    e.matmul(o[:, h2 * 129:h2 * 129 + 129], l, r,
                             start=st, stop=sp),
                    waits=[tk2, tv] + ([prev_kv[sl % 2]]
                                       if ts == 0 and h2 == 0 and
                                       prev_kv[sl % 2] else []),
                    inc=S.pe if (ts == 15 and h2 == 1) else None)
            k_gate = kv_last
            v_gate = kv_last
        tke = p.emit("scalar", lambda e, i=kvb,
                     o=aperm.kvacc[:, sl * 258:(sl + 1) * 258]: e.activation(
            out=o, in_=i[:, 0:258], func=ACT.Copy), waits=[kv_last], inc=S.act)
        prev_kv[sl % 2] = tke
        kv_ev.append(tke)
        slab_gate = mmk
        bks_gate = tk1 if True else None
        bks_gate = kv_last
    ctx["tok_kv"] = kv_ev[-2:]


def _emit_allreduce(ctx):
    p, S = ctx["p"], ctx["S"]
    aperm = ctx["aperm"]
    t1 = p.dma("sync", ctx["cc_in"], aperm.kvacc[:, 0:H * 129],
               waits=ctx["tok_kv"])
    tcc = p.emit("gpsimd", lambda e: e.collective_compute(
        "AllReduce", ALU.add, ins=[ctx["cc_in"].opt()],
        outs=[ctx["cc_out"].opt()], replica_groups=[list(range(NCORES))]),
        waits=[t1], inc=S.cc)
    t2 = p.dma("sync", aperm.kvacc[:, 0:H * 129], ctx["cc_out"], waits=[tcc])
    ctx["tok_kv_all"] = t2


def _emit_phase2(ctx):
    p, S = ctx["p"], ctx["S"]
    ws2 = ctx["ws2"]
    psum = ctx["psum"]
    aperm = ctx["aperm"]
    pf = ctx["psum_free"]
    dep_x = ctx["tok_xln"]
    kv_t = ctx["tok_kv_all"]
    NB = 4
    BW = 512
    wq_gate = None
    xc_state = [[None, None], [None, None]]
    o_sp = []
    pv = {}
    for e8 in range(8):
        sdep = ctx["tok_wq_pair"] + ([wq_gate] if wq_gate else [])
        tq1 = p.dma("sync", ws2.wqs[:, 0:4096], ctx["wq_pair"][0, e8],
                    waits=sdep)
        tq2 = p.dma("sync", ws2.wqs[:, 4096:8192], ctx["wq_pair"][1, e8],
                    waits=[tq1])
        for b in range(NB):
            psQ = [psum[0], psum[1]]
            mml = None
            for k in range(C2):
                xb = k % 2
                xh_c = ws2.xc2[:, xb * 1024:xb * 1024 + 512]
                xl_c = ws2.xc2[:, xb * 1024 + 512:xb * 1024 + 1024]
                xdep = dep_x + [t for t in xc_state[xb] if t]
                src_h = ctx["xln_h"][:].rearrange(
                    "p (t kc) -> p t kc", t=16)[:, 4 * b:4 * b + 4,
                                                k * 128:(k + 1) * 128]
                src_l = ctx["xln_l"][:].rearrange(
                    "p (t kc) -> p t kc", t=16)[:, 4 * b:4 * b + 4,
                                                k * 128:(k + 1) * 128]
                th_ = p.dma("sync", xh_c.rearrange("p (f c) -> p f c", f=4),
                            src_h, waits=xdep)
                tl_ = p.dma("sync", xl_c.rearrange("p (f c) -> p f c", f=4),
                            src_l, waits=[th_])
                for h2 in range(2):
                    wqh = ws2.wqs[:, k * 256 + h2 * 128:
                                  k * 256 + (h2 + 1) * 128]
                    wql = ws2.wqs[:, 4096 + k * 256 + h2 * 128:
                                  4096 + k * 256 + (h2 + 1) * 128]
                    w0 = [th_, tl_, tq1, tq2]
                    if k == 0:
                        w0 = w0 + [t for t in (pf[0], pf[1]) if t]
                    st = (k == 0)
                    sp = (k == C2 - 1)
                    p.emit("tensor", lambda e, o=psQ[h2], w=wqh, x=xh_c, st=st:
                           e.matmul(o[:, 0:BW], w, x, start=st, stop=False),
                           waits=w0)
                    p.emit("tensor", lambda e, o=psQ[h2], w=wqh, x=xl_c:
                           e.matmul(o[:, 0:BW], w, x, start=False, stop=False))
                    mml = p.emit("tensor", lambda e, o=psQ[h2], w=wql, x=xh_c,
                                 sp=sp: e.matmul(o[:, 0:BW], w, x,
                                                 start=False, stop=sp),
                                 inc=S.pe if (sp and h2 == 1) else None)
                xc_state[xb] = [tl_, mml]
            for h2 in range(2):
                h = e8 * 2 + h2
                qb = ws2.qbuf[:, h2 * 512:(h2 + 1) * 512]
                tqe = p.emit("scalar", lambda e, o=qb, i=psQ[h2],
                             bq=ctx["bq_col"][:, h:h + 1]: e.activation(
                    out=o, in_=i[:, 0:BW], func=ACT.Relu, bias=bq, scale=1.0),
                    waits=[mml, pv.get(("q", h2))] + ctx["tok_bias_list"],
                    inc=S.act)
                pf[h2] = tqe
                ops_ = psum[2 + h2]
                kvl = aperm.kvacc[:, e8 * 258 + h2 * 129:
                                  e8 * 258 + h2 * 129 + 128]
                to_ = p.emit("tensor", lambda e, o=ops_, l=kvl, r=qb:
                             e.matmul(o[:, 0:BW], l, r, start=True, stop=True),
                             waits=[tqe, kv_t, pv.get(("o", h2))],
                             inc=S.pe)
                sps = psum[4 + h2]
                ksl = aperm.kvacc[:, e8 * 258 + h2 * 129 + 128:
                                  e8 * 258 + h2 * 129 + 129]
                ts_ = p.emit("tensor", lambda e, o=sps, l=ksl, r=qb:
                             e.matmul(o[0:1, 0:BW], l, r, start=True,
                                      stop=True),
                             waits=[tqe, kv_t, pv.get(("s", h2))],
                             inc=S.pe)
                srow = ws2.srow2[0:1, h2 * 512:(h2 + 1) * 512]
                tse = p.emit("scalar", lambda e, o=srow, i=sps: e.activation(
                    out=o, in_=i[0:1, 0:BW], func=ACT.Copy),
                    waits=[ts_, pv.get(("sr", h2))], inc=S.act)
                grow = aperm.grow[0:1, b * BW:(b + 1) * BW]
                tg1 = p.emit("vector", lambda e, o=srow, d=grow:
                             e.tensor_tensor(o, o, d, ALU.mult),
                             waits=[tse, ctx["tok_decay_rows"]], inc=S.dve)
                tg2 = p.emit("vector", lambda e, o=srow: e.tensor_scalar(
                    out=o, in0=o, scalar1=EPS_ATTN, scalar2=None, op0=ALU.add),
                    waits=[tg1], inc=S.dve)
                tg3 = p.emit("vector", lambda e, o=srow: e.reciprocal(o, o),
                             waits=[tg2], inc=S.dve)
                tg4 = p.emit("vector", lambda e, o=srow, d=grow:
                             e.tensor_tensor(o, o, d, ALU.mult),
                             waits=[tg3], inc=S.dve)
                pv[("sr", h2)] = tg4
                gbps = psum[6 + h2]
                tb_ = p.emit("tensor", lambda e, o=gbps, l=ctx["ones_col"],
                             r=srow: e.matmul(o[:, 0:BW], l, r, start=True,
                                              stop=True),
                             waits=[tg4, pv.get(("gb", h2))], inc=S.pe)
                gbs = ws2.obuf[:, h2 * 512:(h2 + 1) * 512]
                tbe = p.emit("scalar", lambda e, o=gbs, i=gbps: e.activation(
                    out=o, in_=i[:, 0:BW], func=ACT.Copy),
                    waits=[tb_, pv.get(("gbs", h2))], inc=S.act)
                pv[("gb", h2)] = tbe
                oh = ws2.opr[:, h2 * 1024:h2 * 1024 + 512]
                ol = ws2.opr[:, h2 * 1024 + 512:h2 * 1024 + 1024]
                tof = p.emit("vector", lambda e, o=qb, a=ops_, g=gbs:
                             e.tensor_tensor(o, a[:, 0:BW], g, ALU.mult),
                             waits=[to_, ts_, tbe], inc=S.dve)
                pv[("o", h2)] = tof
                pv[("q", h2)] = tof
                pv[("gbs", h2)] = tof
                toh = p.emit("scalar", lambda e, o=oh, i=qb: e.activation(
                    out=o, in_=i, func=ACT.Copy),
                    waits=[tof, pv.get(("oh", h2))], inc=S.act)
                tol = p.emit("vector", lambda e, o=ol, i=qb, h_=oh:
                             e.tensor_tensor(o, i, h_, ALU.subtract),
                             waits=[toh], inc=S.dve)
                t1 = p.dma("sync", ctx["oj_h"][:, h * T + b * BW:
                                               h * T + (b + 1) * BW], oh,
                           waits=[toh])
                t2 = p.dma("sync", ctx["oj_l"][:, h * T + b * BW:
                                               h * T + (b + 1) * BW], ol,
                           waits=[tol])
                pv[("oh", h2)] = t2
                o_sp.extend([t1, t2])
        wq_gate = mml
    ctx["tok_o"] = o_sp[-4:]


def _emit_wo(ctx):
    p, S = ctx["p"], ctx["S"]
    ws3 = ctx["ws3"]
    psum = ctx["psum"]
    xres = ctx["xres"]
    pf = ctx["psum_free"]
    NB = 4
    BW = 512
    dep_o = ctx["tok_o"]
    dep_w = ctx["tok_wo_pair"]
    ob_gate = None
    wo_state = [[None, None], [None, None]]
    t_last = None
    for b in range(NB):
        src_h = ctx["oj_h"][:].rearrange("p (k t) -> p k t", k=C2)[
            :, :, b * BW:(b + 1) * BW]
        src_l = ctx["oj_l"][:].rearrange("p (k t) -> p k t", k=C2)[
            :, :, b * BW:(b + 1) * BW]
        ob_dep = dep_o + ([ob_gate] if ob_gate else [])
        t_oh = p.dma("sync", ws3.oblk[:, 0:C2 * BW].rearrange(
            "p (k t) -> p k t", k=C2), src_h, waits=ob_dep)
        t_ol = p.dma("sync", ws3.oblk[:, C2 * BW:2 * C2 * BW].rearrange(
            "p (k t) -> p k t", k=C2), src_l, waits=[t_oh])
        mml = None
        for f in range(C2):
            wb = f % 2
            wsl_h = ws3.wos[:, wb * 4096:wb * 4096 + 2048]
            wsl_l = ws3.wos[:, wb * 4096 + 2048:wb * 4096 + 4096]
            wdep = dep_w + [t for t in wo_state[wb] if t]
            tw1 = p.dma("sync", wsl_h, ctx["wo_pair"][0, f], waits=wdep)
            tw2 = p.dma("sync", wsl_l, ctx["wo_pair"][1, f], waits=[tw1])
            ps = psum[f % 2]
            for k in range(C2):
                oh_c = ws3.oblk[:, k * BW:(k + 1) * BW]
                ol_c = ws3.oblk[:, C2 * BW + k * BW:C2 * BW + (k + 1) * BW]
                wh = wsl_h[:, k * 128:(k + 1) * 128]
                wl = wsl_l[:, k * 128:(k + 1) * 128]
                w0 = []
                if k == 0:
                    w0 = [tw1, tw2, t_oh, t_ol] + \
                        ([pf[f % 2]] if pf[f % 2] else [])
                st = (k == 0)
                sp = (k == C2 - 1)
                p.emit("tensor", lambda e, o=ps, w=wh, x=oh_c, st=st:
                       e.matmul(o[:, 0:BW], w, x, start=st, stop=False),
                       waits=w0)
                p.emit("tensor", lambda e, o=ps, w=wh, x=ol_c:
                       e.matmul(o[:, 0:BW], w, x, start=False, stop=False))
                mml = p.emit("tensor", lambda e, o=ps, w=wl, x=oh_c, sp=sp:
                             e.matmul(o[:, 0:BW], w, x, start=False, stop=sp),
                             inc=S.pe if sp else None)
            wo_state[wb] = [tw2, mml]
            xsl = xres[:, f * T + b * BW:f * T + (b + 1) * BW]
            t_last = p.emit("vector", lambda e, x=xsl, i=ps,
                            bo=ctx["bo_col"][:, f:f + 1]:
                            e.scalar_tensor_tensor(out=x, in0=i[:, 0:BW],
                                                   scalar=bo, in1=x,
                                                   op0=ALU.add, op1=ALU.add),
                            waits=[mml, ctx["tok_resid_done"]] +
                                  ctx["tok_bias_list"], inc=S.dve)
            pf[f % 2] = t_last
        ob_gate = mml
    ctx["tok_attn_done"] = t_last


def emit_output(ctx):
    p, S = ctx["p"], ctx["S"]
    dep = [d for d in (ctx.get("tok_resid_done"), ctx.get("tok_attn_done"))
           if d]
    st_toks = []
    for i in range(4):
        sl = slice(i * (C2 * T // 4), (i + 1) * (C2 * T // 4))
        st_toks.append(p.dma("sync", ctx["y_out"][:, sl], ctx["xres"][:, sl],
                             waits=dep))
    p.wait_only("sync", st_toks)
    p.wait_only("gpsimd", st_toks)



def make_inputs(full):
    x = np.asarray(full["L0_attn_x"])[0]
    h128 = _hadamard(128)
    shared = {
        "w_up_t": np.ascontiguousarray(np.asarray(full["L0_ffn_w_up"]).T),
        "w_down_t": np.ascontiguousarray(np.asarray(full["L0_ffn_w_down"]).T),
        "wq_t": np.ascontiguousarray(np.asarray(full["L0_attn_Wq"]).T),
        "wk_t": np.ascontiguousarray(np.asarray(full["L0_attn_Wk"]).T),
        "wv_t": np.ascontiguousarray(np.asarray(full["L0_attn_Wv"]).T),
        "wo_t": np.ascontiguousarray(np.asarray(full["L0_attn_Wo"]).T),
        "bq_col": col_layout(full["L0_attn_bq"]),
        "bo_col": col_layout(full["L0_attn_bo"]),
        "bk_rep": np.ascontiguousarray(
            np.broadcast_to(np.asarray(full["L0_attn_bk"]), (128, D))),
        "bv_rep": np.ascontiguousarray(
            np.broadcast_to(np.asarray(full["L0_attn_bv"]), (128, D))),
        "h128_in": h128,
    }
    maps = []
    for c in range(NCORES):
        pos = np.arange(c * T, (c + 1) * T, dtype=np.float32)
        m = dict(shared)
        m["x_fm"] = fm_layout(x[c * T:(c + 1) * T])
        m["pos_col"] = np.ascontiguousarray(pos.reshape(C2, 128).T)
        m["pos_row"] = np.ascontiguousarray(np.broadcast_to(pos, (2, T)).copy())
        maps.append(m)
    return maps


def kernel(**inputs):
    nc = build_program()
    maps = make_inputs(inputs)
    res = run_bass_kernel_spmd(nc, maps, core_ids=list(range(NCORES)))
    outs = [fm_unlayout(res.results[c]["y_fm"], T, D) for c in range(NCORES)]
    return np.concatenate(outs, axis=0)[None].astype(np.float32)


# version 03
# speedup vs baseline: 1.1565x; 1.1565x over previous
import math
import numpy as np
import ml_dtypes
import concourse.bass as bass
import concourse.bacc as bacc
import concourse.mybir as mybir
from concourse.bass_isa import ReduceOp
from concourse.bass_utils import run_bass_kernel_spmd

F32 = mybir.dt.float32
BF16 = mybir.dt.bfloat16
I8 = mybir.dt.int8
ALU = mybir.AluOpType
ACT = mybir.ActivationFunctionType
AXX = mybir.AxisListType.X

NCORES = 8
D = 2048
T = 2048
NSEQ = 16384
C2 = 16
C4 = 32
SB = 256
NSTRIPE = T // SB
H = 16
EPS_ATTN = 1e-6
EPS_LN = 1e-5
EPS_Q = 1e-5
QB = 127.0
RT2048 = math.sqrt(2048.0)
RT4096 = math.sqrt(4096.0)

ENGINES = ("tensor", "vector", "scalar", "gpsimd", "sync")

PARTS = ("ffn1", "attn", "ffn2")


class Sem:
    __slots__ = ("h", "n", "name")

    def __init__(self, handle, name):
        self.h = handle
        self.n = 0
        self.name = name


class P:
    def __init__(self, nc, block):
        self.nc = nc
        self.block = block
        self.ops = {e: [] for e in ENGINES}
        self.log = {e: [] for e in ENGINES}
        self.dma_pool = []
        self.dma_rr = 0

    def init_dma_pool(self, sems):
        self.dma_pool = sems

    def emit(self, eng, fn, waits=(), inc=None, inc_k=1):
        best = {}
        for tok in waits:
            if tok is None:
                continue
            s, v = tok
            if v > best.get(s, -1):
                best[s] = v
        for s, v in best.items():
            if v > s.n:
                raise RuntimeError(f"wait {s.name}>={v} but emitted {s.n}")
            self.ops[eng].append(("w", s, v))
            self.log[eng].append(("w", s, v))
        tok = None
        if inc is not None:
            inc.n += inc_k
            tok = (inc, inc.n)
        self.ops[eng].append(("op", fn, inc, inc_k))
        self.log[eng].append(("op", None, inc, inc_k))
        return tok

    def wait_only(self, eng, waits):
        best = {}
        for tok in waits:
            if tok is None:
                continue
            s, v = tok
            if v > best.get(s, -1):
                best[s] = v
        for s, v in best.items():
            self.ops[eng].append(("w", s, v))
            self.log[eng].append(("w", s, v))

    def dma(self, eng, out, in_, waits=(), inc=None, spread=False, **kw):
        if spread and eng == "sync":
            self.spread_rr = getattr(self, "spread_rr", 0) ^ 1
            eng = ("sync", "scalar")[self.spread_rr]
        s = self.dma_pool[self.dma_rr]
        self.dma_rr = (self.dma_rr + 1) % len(self.dma_pool)
        gate = [(s, s.n)] if s.n else []
        return self.emit(eng, lambda e: e.dma_start(out=out, in_=in_, **kw),
                         waits=list(waits) + gate, inc=s, inc_k=16)

    def flush(self):
        pend = self.ops
        self.ops = {e: [] for e in ENGINES}

        def mk(ops):
            def body(eng):
                for item in ops:
                    if item[0] == "w":
                        eng.wait_ge(item[1].h, item[2])
                    else:
                        _, fn, inc, k = item
                        ins = fn(eng)
                        if inc is not None:
                            ins.then_inc(inc.h, k)
            return body
        for e in ENGINES:
            if pend[e]:
                getattr(self.block, e)(mk(pend[e]))

    def check(self):
        pos = {e: 0 for e in ENGINES}
        cnt = {}
        progress = True
        while progress:
            progress = False
            for e in ENGINES:
                while pos[e] < len(self.log[e]):
                    item = self.log[e][pos[e]]
                    if item[0] == "w":
                        if cnt.get(item[1], 0) < item[2]:
                            break
                    else:
                        _, _, inc, k = item
                        if inc is not None:
                            cnt[inc] = cnt.get(inc, 0) + k
                    pos[e] += 1
                    progress = True
        stuck = {}
        for e in ENGINES:
            if pos[e] < len(self.log[e]):
                it = self.log[e][pos[e]]
                stuck[e] = (f"idx {pos[e]}/{len(self.log[e])} wait "
                            f"{it[1].name}>={it[2]} have {cnt.get(it[1], 0)}")
        if stuck:
            raise RuntimeError(f"schedule deadlock: {stuck}")


class SlotTracker:
    def __init__(self, n):
        self.w = [None] * n
        self.r = [[] for _ in range(n)]

    def read_waits(self, idxs):
        return [self.w[i] for i in idxs if self.w[i] is not None]

    def note_read(self, idxs, tok):
        for i in idxs:
            self.r[i].append(tok)

    def write_waits(self, idxs):
        out = []
        for i in idxs:
            if self.w[i] is not None:
                out.append(self.w[i])
            out.extend(self.r[i])
        return out

    def note_write(self, idxs, tok):
        for i in idxs:
            self.w[i] = tok
            self.r[i] = []



def fm_layout(x_tok_major):
    Tl, Dl = x_tok_major.shape
    a = x_tok_major.reshape(Tl, Dl // 128, 128).transpose(2, 1, 0)
    return np.ascontiguousarray(a.reshape(128, (Dl // 128) * Tl))


def fm_unlayout(a, Tl, Dl):
    x = a.reshape(128, Dl // 128, Tl).transpose(2, 1, 0)
    return np.ascontiguousarray(x.reshape(Tl, Dl))


def col_layout(v):
    return np.ascontiguousarray(np.asarray(v).reshape(-1, 128).T)


def _hadamard(n):
    h = np.array([[1.0]], dtype=np.float32)
    while h.shape[0] < n:
        h = np.block([[h, h], [h, -h]])
    return h.astype(np.float32)



def build_program(parts=None):
    if parts is None:
        parts = PARTS
    nc = bacc.Bacc("TRN2", target_bir_lowering=False)
    ctx = {"nc": nc, "parts": parts, "psum_free": [None] * 8}

    ctx["x_in"] = nc.dram_tensor("x_fm", [128, C2 * T], F32, kind="ExternalInput").ap()
    ctx["y_out"] = nc.dram_tensor("y_fm", [128, C2 * T], F32, kind="ExternalOutput").ap()
    ctx["wup_t"] = nc.dram_tensor("w_up_t", [D, 4096], F32, kind="ExternalInput").ap()
    ctx["wdn_t"] = nc.dram_tensor("w_down_t", [4096, D], F32, kind="ExternalInput").ap()
    for nm in ("wq_t", "wk_t", "wv_t", "wo_t"):
        ctx[nm] = nc.dram_tensor(nm, [D, D], F32, kind="ExternalInput").ap()
    ctx["bq_col_in"] = nc.dram_tensor("bq_col", [128, C2], F32, kind="ExternalInput").ap()
    ctx["bo_col_in"] = nc.dram_tensor("bo_col", [128, C2], F32, kind="ExternalInput").ap()
    ctx["bk_rep"] = nc.dram_tensor("bk_rep", [128, D], F32, kind="ExternalInput").ap()
    ctx["bv_rep"] = nc.dram_tensor("bv_rep", [128, D], F32, kind="ExternalInput").ap()
    ctx["pos_col"] = nc.dram_tensor("pos_col", [128, C2], F32, kind="ExternalInput").ap()
    ctx["pos_row"] = nc.dram_tensor("pos_row", [2, T], F32, kind="ExternalInput").ap()
    ctx["h128_in"] = nc.dram_tensor("h128_in", [128, 128], F32, kind="ExternalInput").ap()

    ctx["ternu"] = nc.dram_tensor("ternu", [C4, 128, C2 * 128], BF16, kind="Internal").ap()
    ctx["ternd"] = nc.dram_tensor("ternd", [C2, 128, C4 * 128], BF16, kind="Internal").ap()
    if "attn" in parts:
        for nm, shp in (("wk_pair", [2, 8, 128, C2 * 256]),
                        ("wv_pair", [2, 8, 128, C2 * 256]),
                        ("wq_pair", [2, 8, 128, C2 * 256]),
                        ("wo_pair", [2, C2, 128, C2 * 128]),
                        ("xln_h", [128, C2 * T]), ("xln_l", [128, C2 * T]),
                        ("oj_h", [128, C2 * T]), ("oj_l", [128, C2 * T])):
            ctx[nm] = nc.dram_tensor(nm, shp, BF16, kind="Internal").ap()
        ctx["cc_in"] = nc.dram_tensor("cc_in", [128, H * 129], F32, kind="Internal").ap()
        ctx["cc_out"] = nc.dram_tensor("cc_out", [128, H * 129], F32, kind="Internal").ap()

    with (
        nc.sbuf_tensor("xres", [128, C2 * T], F32) as xres,
        nc.sbuf_tensor("h128t", [128, 128], BF16) as h128,
        nc.sbuf_tensor("consts", [128, 64], F32) as consts,
        nc.sbuf_tensor("smalls", [128, 1344], F32) as smalls,
        nc.Block() as block,
        _sems(nc) as S,
    ):
        pool_cms = [nc.semaphore(f"dmap{i}") for i in range(32)]
        pool = [Sem(cm.__enter__(), f"dmap{i}") for i, cm in enumerate(pool_cms)]
        psum_g = [nc.psum_tensor(f"ps{i}", [128, 512], F32) for i in range(8)]
        psum = [g.__enter__() for g in psum_g]
        try:
            p = P(nc, block)
            p.init_dma_pool(pool)
            ctx.update(p=p, S=S, psum=psum, xres=xres, h128=h128,
                       consts=consts, smalls=smalls)
            emit_setup(ctx)
            with _ffn_ws(nc) as ws:
                ctx["ffn"] = ws
                emit_ffn_prep(ctx)
                if "ffn1" in parts:
                    emit_ffn(ctx, "f1")
                p.flush()
            if "attn" in parts:
                emit_attn(ctx)
            if "ffn2" in parts:
                with _ffn_ws(nc) as ws:
                    ctx["ffn"] = ws
                    emit_ffn(ctx, "f2")
                    p.flush()
            emit_output(ctx)
            p.flush()
            p.check()
        finally:
            for g in reversed(psum_g):
                g.__exit__(None, None, None)
            for cm in reversed(pool_cms):
                cm.__exit__(None, None, None)
    nc.finalize()
    return nc


class _sems:
    NAMES = ("dw", "dx", "pe", "dve", "act", "gp", "st", "cc", "bar")

    def __init__(self, nc):
        self.nc = nc
        self.cms = []

    def __enter__(self):
        class Bag:
            pass
        bag = Bag()
        for nm in self.NAMES:
            cm = self.nc.semaphore(nm)
            setattr(bag, nm, Sem(cm.__enter__(), nm))
            self.cms.append(cm)
        return bag

    def __exit__(self, *a):
        for cm in reversed(self.cms):
            cm.__exit__(*a)
        return False


_WS_ID = [0]


class _ffn_ws:
    def __init__(self, nc):
        self.nc = nc
        self.cms = []

    def __enter__(self):
        class WS:
            pass
        ws = WS()
        _WS_ID[0] += 1
        i = _WS_ID[0]
        for nm, shape, dt in (
                ("wkW", [128, 32 * SB], F32),
                ("q16", [128, C4 * SB], BF16),
                ("qi8", [128, 8 * SB], I8),
                ("cg16", [128, 8 * SB], BF16),
                ("bct", [128, 8 * SB], F32),
                ("wslab", [128, 6144], BF16),
        ):
            cm = self.nc.sbuf_tensor(f"{nm}_{i}", shape, dt)
            setattr(ws, nm, cm.__enter__())
            self.cms.append(cm)
        return ws

    def __exit__(self, *a):
        for cm in reversed(self.cms):
            cm.__exit__(*a)
        return False



def emit_setup(ctx):
    p, S = ctx["p"], ctx["S"]
    xres, consts, smalls = ctx["xres"], ctx["consts"], ctx["smalls"]
    xt = []
    for i in range(4):
        sl = slice(i * (C2 * T // 4), (i + 1) * (C2 * T // 4))
        xt.append(p.dma("sync", xres[:, sl], ctx["x_in"][:, sl]))
    ctx["tok_x_list"] = xt
    ctx["tok_x"] = None
    th = p.dma("sync", smalls[:, 0:128], ctx["h128_in"])
    ctx["tok_h128"] = p.emit(
        "vector", lambda e: e.tensor_copy(ctx["h128"][:], smalls[:, 0:128]),
        waits=[th], inc=S.dve)

    tp_ = p.dma("sync", consts[:, 0:C2], ctx["pos_col"])
    ctx["tok_decay_col"] = p.emit(
        "scalar", lambda e: e.activation(out=consts[:, 0:C2],
                                         in_=consts[:, 0:C2],
                                         func=ACT.Exp, scale=-1.0 / NSEQ),
        waits=[tp_], inc=S.act)
    ctx["decay_col"] = consts[:, 0:C2]
    tb1 = p.dma("sync", consts[:, 16:32], ctx["bq_col_in"])
    tb2 = p.dma("sync", consts[:, 32:48], ctx["bo_col_in"])
    ctx["bq_col"] = consts[:, 16:32]
    ctx["bo_col"] = consts[:, 32:48]
    ctx["tok_bias_list"] = [tb1, tb2]



def emit_ffn_prep(ctx):
    p, S = ctx["p"], ctx["S"]
    ws = ctx["ffn"]
    wk = ws.wkW
    consts = ctx["consts"]

    prev_red = None
    for wi, (name, wt, nk, ncols) in enumerate(
            (("wu", ctx["wup_t"], C2, 4096), ("wd", ctx["wdn_t"], C4, D))):
        nj = ncols // 2048
        nt = nk * nj
        red = ctx["smalls"][:, 128:128 + nt]
        toks = []
        red_toks = []
        for it in range(nt):
            k, j = divmod(it, nj)
            buf = wk[:, (it % 2) * 2048:(it % 2) * 2048 + 2048]
            w_dep = [] if it < 2 else [red_toks[it - 2]]
            if it < 2 and prev_red is not None:
                w_dep = [prev_red]
            tl = p.dma("sync", buf,
                       wt[k * 128:(k + 1) * 128, j * 2048:(j + 1) * 2048],
                       waits=w_dep, spread=True)
            tr = p.emit("vector", lambda e, b=buf, r=red[:, it:it + 1]:
                        e.tensor_reduce(r, b, AXX, ALU.add,
                                        apply_absolute_value=True),
                        waits=[tl], inc=S.dve)
            toks.append(tl)
            red_toks.append(tr)
        tot = consts[:, 56 + wi:57 + wi]
        tf = p.emit("vector", lambda e, r=red, t=tot: e.tensor_reduce(
            t, r, AXX, ALU.add), waits=[red_toks[-1]], inc=S.dve)
        prev_red = tf
        tpar = p.emit("gpsimd", lambda e, t=tot: e.partition_all_reduce(
            t, t, 128, ReduceOp.add), waits=[tf], inc=S.gp)
        cnt = float(2048 * 4096)
        sc = consts[:, 48 + wi * 2:49 + wi * 2]
        th = consts[:, 49 + wi * 2:50 + wi * 2]
        nth = consts[:, 54 + wi:55 + wi]
        t1 = p.emit("vector", lambda e, t=tot, s=sc: e.tensor_scalar(
            out=s, in0=t, scalar1=1.0 / cnt, scalar2=EPS_Q,
            op0=ALU.mult, op1=ALU.max), waits=[tpar], inc=S.dve)
        t2 = p.emit("vector", lambda e, s=sc, t=th: e.tensor_scalar(
            out=t, in0=s, scalar1=0.5, scalar2=None, op0=ALU.mult),
            waits=[t1], inc=S.dve)
        t3 = p.emit("vector", lambda e, s=sc, n=nth: e.tensor_scalar(
            out=n, in0=s, scalar1=-0.5, scalar2=None, op0=ALU.mult),
            waits=[t1], inc=S.dve)
        ctx[f"{name}_scale"] = sc
        ctx[f"{name}_thr"] = (th, nth, t3 if t3[1] > t2[1] else t2)
        ctx[f"{name}_thr_toks"] = (t2, t3)

    for name, wt, tern_d, nk, ncols in (
            ("wu", ctx["wup_t"], ctx["ternu"], C2, 4096),
            ("wd", ctx["wdn_t"], ctx["ternd"], C4, D)):
        th, nth, _ = ctx[f"{name}_thr"]
        t2, t3 = ctx[f"{name}_thr_toks"]
        nj = ncols // 2048
        nslab = ncols // 128
        toks = list(ctx.get("tern_carry_dma", []))
        carry_tt = ctx.get("tern_carry_tt")
        ncar = len(toks)
        for it0 in range(nk * nj):
            k, j = divmod(it0, nj)
            sl = it0 % 2
            buf = wk[:, 4096 + sl * 2048:4096 + sl * 2048 + 2048]
            lbuf = wk[:, sl * 2048:sl * 2048 + 2048]
            tbuf = ws.q16[:, sl * 2048:sl * 2048 + 2048]
            it2 = it0 + ncar
            w_dep = ([t2, t3, prev_red, carry_tt] if it2 < 2 else
                     [toks[it2 - 2]])
            tl = p.dma("sync", buf,
                       wt[k * 128:(k + 1) * 128, j * 2048:(j + 1) * 2048],
                       waits=w_dep, spread=True)
            tlt = p.emit("vector", lambda e, b=buf, l=lbuf, n=nth:
                         e.tensor_scalar(out=l, in0=b, scalar1=n[:, 0:1],
                                         scalar2=None, op0=ALU.is_lt),
                         waits=[tl, t3] + ([toks[it2 - 2]] if it2 >= 2 else []),
                         inc=S.dve)
            tt = p.emit("vector", lambda e, b=buf, l=lbuf, t=tbuf, h=th:
                        e.scalar_tensor_tensor(out=t, in0=b, scalar=h[:, 0:1],
                                               in1=l, op0=ALU.is_gt,
                                               op1=ALU.subtract),
                        waits=[tlt, t2], inc=S.dve)
            nsl = 2048 // 128
            src_ = tbuf.rearrange("p (j c) -> p j c", j=nsl)
            dst = tern_d[j * nsl:(j + 1) * nsl, :, k * 128:(k + 1) * 128] \
                .transpose([1, 0, 2])
            tw = p.dma("sync", dst, src_, waits=[tt], spread=True)
            toks.append(tw)
        ctx[f"tok_tern_{name}"] = toks[-1]
        ctx["tern_carry_dma"] = toks[-2:]
        ctx["tern_carry_tt"] = tt
    ctx["tok_prep_q16"] = None
    ctx["tok_prep_q16_l"] = list(ctx["tern_carry_dma"])




def _tap(*a, **k):
    return


def _wsl(ws, i):
    return ws.wkW[:, i * SB:i * SB + SB]


def emit_ffn(ctx, tag):
    ctx["ffn_slots"] = SlotTracker(32)
    st_prev = None
    for s in range(NSTRIPE):
        if st_prev is not None:
            _ffn_down(ctx, st_prev, tag)
        st = _ffn_stageA(ctx, s, tag)
        _ffn_quant1(ctx, st, tag)
        _ffn_upmid(ctx, st, tag)
        st_prev = st
    _ffn_down(ctx, st_prev, tag)
    ctx["tok_resid_done"] = ctx["tok_down_last"]
    if tag == "f2":
        ctx["tok_final"] = ctx["tok_resid_done"]


def _ffn_stageA(ctx, s, tag):
    p, S = ctx["p"], ctx["S"]
    ws = ctx["ffn"]
    xres = ctx["xres"]
    psum = ctx["psum"]
    h128 = ctx["h128"]
    pf = ctx["psum_free"]
    t0 = s * SB
    wk = ws.wkW
    slots = ctx["ffn_slots"]
    st = {"s": s, "t0": t0}

    def xs(c):
        return xres[:, c * T + t0: c * T + t0 + SB]
    st["xs"] = xs
    x_stripe = xres[:].rearrange("p (c t) -> p c t", c=C2)[:, :, t0:t0 + SB]

    if tag == "f1":
        dep_x = list(ctx["tok_x_list"])
        if s == 0:
            dep_x.extend(ctx.get("tok_prep_q16_l", []))
    else:
        dep_x = [ctx["tok_attn_done"]]
    st["dep_x"] = dep_x

    sq_out = wk[:, 16 * SB:32 * SB].rearrange("p (c t) -> p c t", c=C2)
    t_sq = p.emit("scalar", lambda e: e.activation(
        out=sq_out, in_=x_stripe, func=ACT.Square),
        waits=dep_x + slots.write_waits(range(16, 32)), inc=S.act)
    slots.note_write(range(16, 32), t_sq)

    def fold_region(src_base, n, tok):
        cur = n
        while cur > 1:
            hh = cur // 2
            a = wk[:, src_base * SB:(src_base + hh) * SB] \
                .rearrange("p (c t) -> p c t", c=hh)
            b = wk[:, (src_base + hh) * SB:(src_base + cur) * SB] \
                .rearrange("p (c t) -> p c t", c=hh)
            tok = p.emit("vector", lambda e, a=a, b=b:
                         e.tensor_tensor(a, a, b, ALU.add),
                         waits=[tok], inc=S.dve)
            cur = hh
        return tok

    a0 = wk[:, 0:8 * SB].rearrange("p (c t) -> p c t", c=8)
    t_fx = p.emit("vector", lambda e: e.tensor_tensor(
        a0, x_stripe[:, 0:8, :], x_stripe[:, 8:16, :], ALU.add),
        waits=dep_x + slots.write_waits(range(0, 8)), inc=S.dve)
    slots.note_write(range(0, 8), t_fx)
    t_sx = fold_region(0, 8, t_fx)
    slots.note_write([0], t_sx)
    slots.note_read(range(0, 8), t_sx)
    t_sxx = fold_region(16, 16, t_sq)
    slots.note_write([16], t_sxx)
    slots.note_read(range(16, 32), t_sxx)

    bct = ws.bct
    b_sx = bct[:, 0 * SB:1 * SB]
    b_sxx = bct[:, 1 * SB:2 * SB]
    t1 = p.emit("gpsimd", lambda e: e.partition_all_reduce(
        b_sx, wk[:, 0:SB], 128, ReduceOp.add), waits=[t_sx], inc=S.gp)
    slots.note_read([0], t1)
    t2 = p.emit("gpsimd", lambda e: e.partition_all_reduce(
        b_sxx, wk[:, 16 * SB:17 * SB], 128, ReduceOp.add),
        waits=[t_sxx], inc=S.gp)
    slots.note_read([16], t2)

    b_mu = bct[:, 2 * SB:3 * SB]
    b_r = bct[:, 3 * SB:4 * SB]
    b_t1 = bct[:, 4 * SB:5 * SB]
    b_t2 = bct[:, 5 * SB:6 * SB]
    tm = p.emit("vector", lambda e: e.tensor_scalar(
        out=b_mu, in0=b_sx, scalar1=1.0 / D, scalar2=None, op0=ALU.mult),
        waits=[t1], inc=S.dve)
    tv1 = p.emit("vector", lambda e: e.tensor_scalar(
        out=b_t1, in0=b_sxx, scalar1=1.0 / D, scalar2=None, op0=ALU.mult),
        waits=[t2], inc=S.dve)
    tv2 = p.emit("vector", lambda e: e.tensor_tensor(b_t2, b_mu, b_mu, ALU.mult),
                 waits=[tm], inc=S.dve)
    tv3 = p.emit("vector", lambda e: e.scalar_tensor_tensor(
        out=b_t1, in0=b_t1, scalar=EPS_LN, in1=b_t2,
        op0=ALU.add, op1=ALU.subtract), waits=[tv1, tv2], inc=S.dve)
    ts1 = p.emit("scalar", lambda e: e.activation(
        out=b_t2, in_=b_t1, func=ACT.Sqrt), waits=[tv3], inc=S.act)
    t_r = p.emit("vector", lambda e: e.reciprocal(b_r, b_t2),
                 waits=[ts1], inc=S.dve)

    cg = ws.cg16
    t_h = ctx["tok_h128"]
    ev_toks = []
    for c in range(C2):
        xh = cg[:, (c % 2) * SB:(c % 2) * SB + SB]
        xl = cg[:, (2 + c % 2) * SB:(2 + c % 2) * SB + SB]
        w_prev = [ev_toks[c - 2]] if c >= 2 else []
        th_ = p.emit("scalar", lambda e, o=xh, i=xs(c): e.activation(
            out=o, in_=i, func=ACT.Copy),
            waits=dep_x + w_prev + [ctx.get("tok_prev_q2")], inc=S.act)
        tl_ = p.emit("vector", lambda e, o=xl, i=xs(c), h=xh:
                     e.tensor_tensor(o, i, h, ALU.subtract),
                     waits=[th_], inc=S.dve)
        ps = psum[c % 2]
        mm_dep = [th_, tl_, t_h, pf[c % 2]]
        p.emit("tensor", lambda e, ps=ps, x=xh: e.matmul(
            ps[:, 0:SB], h128[:], x, start=True, stop=False), waits=mm_dep)
        tmm = p.emit("tensor", lambda e, ps=ps, x=xl: e.matmul(
            ps[:, 0:SB], h128[:], x, start=False, stop=True), inc=S.pe)
        dst = c
        tev = p.emit("scalar", lambda e, ps=ps, o=_wsl(ws, dst): e.activation(
            out=o, in_=ps[:, 0:SB], func=ACT.Copy),
            waits=[tmm] + slots.write_waits([dst]), inc=S.act)
        slots.note_write([dst], tev)
        pf[c % 2] = tev
        ev_toks.append(tev)

    tok_b = _butterflies(ctx, slots, C2, ws, eng0=0)
    tz = p.emit("vector", lambda e, a=wk[0:1, 0:SB]:
                e.memset(a, 0.0), waits=[tok_b] + slots.write_waits([0]),
                inc=S.dve)
    slots.note_write([0], tz)

    b_mw = bct[:, 6 * SB:7 * SB]
    t_mw = _absmax(ctx, slots, C2, b_mw, tz)
    b_F = bct[:, 2 * SB:3 * SB]
    b_A1 = bct[:, 0 * SB:1 * SB]
    tq1 = p.emit("vector", lambda e: e.tensor_tensor(b_t1, b_r, b_mw, ALU.mult),
                 waits=[t_mw, t_r], inc=S.dve)
    tq2 = p.emit("vector", lambda e: e.tensor_scalar(
        out=b_t1, in0=b_t1, scalar1=EPS_Q * RT2048, scalar2=None, op0=ALU.max),
        waits=[tq1], inc=S.dve)
    tq3 = p.emit("vector", lambda e: e.reciprocal(b_t2, b_t1),
                 waits=[tq2], inc=S.dve)
    st["t_F"] = p.emit("vector", lambda e: e.scalar_tensor_tensor(
        out=b_F, in0=b_t2, scalar=QB, in1=b_r, op0=ALU.mult, op1=ALU.mult),
        waits=[tq3], inc=S.dve)
    st["t_A1"] = p.emit("vector", lambda e: e.tensor_scalar(
        out=b_A1, in0=b_t1, scalar1=ctx["wu_scale"][:, 0:1],
        scalar2=1.0 / (QB * RT2048), op0=ALU.mult, op1=ALU.mult),
        waits=[tq2], inc=S.dve)
    st["b_F"] = b_F
    st["b_A1"] = b_A1
    st["b_t1"] = b_t1
    st["b_t2"] = b_t2
    return st


def _ffn_quant1(ctx, st, tag):
    slots = ctx["ffn_slots"]
    st["t_q1"] = _quant(ctx, slots, st["b_F"], st["t_F"], n=C2,
                       extra=[t for t in ctx.get("tok_down_q16", []) if t])


def _ffn_upmid(ctx, st, tag):
    p, S = ctx["p"], ctx["S"]
    ws = ctx["ffn"]
    psum = ctx["psum"]
    h128 = ctx["h128"]
    smalls = ctx["smalls"]
    pf = ctx["psum_free"]
    cg = ws.cg16
    wk = ws.wkW
    slots = ctx["ffn_slots"]
    bct = ws.bct
    t_q1 = st["t_q1"]
    t_h = ctx["tok_h128"]
    b_t1, b_t2, b_A1 = st["b_t1"], st["b_t2"], st["b_A1"]

    wslab = ws.wslab
    up_dep = [ctx["tok_tern_wu"], t_q1]
    wst = ctx.setdefault("wslab_state", [[None, None] for _ in range(6)])
    trtok = ctx.setdefault("up_trtok", [None, None])
    ggtok = ctx.setdefault("up_ggtok", [None, None])
    ghtok = ctx.setdefault("up_ghtok", [None, None])
    for m in range(C4):
        ps = psum[m % 2]
        ps2 = psum[2 + m % 2]
        tmm = None
        for kh in range(2):
            sl6 = ctx.get("wslab_ptr", 0)
            ctx["wslab_ptr"] = (sl6 + 1) % 6
            hb = wslab[:, sl6 * 1024:sl6 * 1024 + 1024]
            t_sl = p.dma("sync", hb,
                         ctx["ternu"][m][:, kh * 1024:(kh + 1) * 1024],
                         waits=up_dep + [t for t in wst[sl6] if t])
            wst[sl6] = [t_sl, wst[sl6][1]]
            for kk in range(8):
                k = kh * 8 + kk
                q_c = ws.q16[:, k * SB:(k + 1) * SB]
                extra = [t_sl, t_q1] + ([pf[m % 2]] if k == 0 else [])
                tmm = p.emit("tensor", lambda e, ps=ps,
                             w=hb[:, kk * 128:(kk + 1) * 128], q=q_c,
                             st=(k == 0), sp=(k == C2 - 1): e.matmul(
                    ps[:, 0:SB], w, q, start=st, stop=sp),
                    waits=extra if kk == 0 else [],
                    inc=S.pe if kk == 7 else None)
            wst[sl6] = [wst[sl6][0], tmm]
        tr_ = smalls[:, (m % 2) * SB:(m % 2) * SB + SB]
        gg = smalls[:, 512 + (m % 2) * SB:512 + (m % 2) * SB + SB]
        gh = cg[:, (m % 2) * SB:(m % 2) * SB + SB]
        gl = cg[:, (2 + m % 2) * SB:(2 + m % 2) * SB + SB]
        ta = p.emit("scalar", lambda e, ps=ps, o=tr_: e.activation(
            out=o, in_=ps[:, 0:SB], func=ACT.Relu),
            waits=[tmm, trtok[m % 2]], inc=S.act)
        pf[m % 2] = ta
        tb = p.emit("vector", lambda e, o=gg, t=tr_: e.tensor_tensor(
            o, t, t, ALU.mult), waits=[ta, ggtok[m % 2]], inc=S.dve)
        trtok[m % 2] = tb
        tc = p.emit("scalar", lambda e, o=gh, g=gg: e.activation(
            out=o, in_=g, func=ACT.Copy), waits=[tb, ghtok[m % 2]], inc=S.act)
        td = p.emit("vector", lambda e, o=gl, g=gg, h=gh: e.tensor_tensor(
            o, g, h, ALU.subtract), waits=[tc], inc=S.dve)
        ggtok[m % 2] = td
        mm2_dep = [tc, td, t_h, pf[2 + m % 2]]
        p.emit("tensor", lambda e, ps=ps2, g=gh: e.matmul(
            ps[:, 0:SB], h128[:], g, start=True, stop=False), waits=mm2_dep)
        tm2 = p.emit("tensor", lambda e, ps=ps2, g=gl: e.matmul(
            ps[:, 0:SB], h128[:], g, start=False, stop=True), inc=S.pe)
        dst = m
        tev = p.emit("scalar", lambda e, ps=ps2, o=_wsl(ws, dst): e.activation(
            out=o, in_=ps[:, 0:SB], func=ACT.Copy),
            waits=[tm2] + slots.write_waits([dst]), inc=S.act)
        slots.note_write([dst], tev)
        pf[2 + m % 2] = tev
        ghtok[m % 2] = tm2

    tok_b4 = _butterflies(ctx, slots, C4, ws, eng0=1)
    b_mg = bct[:, 6 * SB:7 * SB]
    t_mg = _absmax(ctx, slots, C4, b_mg, tok_b4)
    b_G = bct[:, 7 * SB:8 * SB]
    tg1 = p.emit("vector", lambda e: e.tensor_tensor(b_t2, b_A1, b_A1, ALU.mult),
                 waits=[st["t_A1"], t_mg], inc=S.dve)
    tg2 = p.emit("vector", lambda e: e.tensor_tensor(b_t1, b_t2, b_mg, ALU.mult),
                 waits=[tg1], inc=S.dve)
    tg3 = p.emit("vector", lambda e: e.tensor_scalar(
        out=b_t1, in0=b_t1, scalar1=EPS_Q * RT4096, scalar2=None, op0=ALU.max),
        waits=[tg2], inc=S.dve)
    tg4 = p.emit("vector", lambda e: e.reciprocal(b_G, b_t1),
                 waits=[tg3], inc=S.dve)
    t_G = p.emit("vector", lambda e: e.scalar_tensor_tensor(
        out=b_G, in0=b_G, scalar=QB, in1=b_t2, op0=ALU.mult, op1=ALU.mult),
        waits=[tg4], inc=S.dve)
    b_A2h = ctx["smalls"][:, 1024:1280]
    st["t_A2h"] = p.emit("vector", lambda e: e.tensor_scalar(
        out=b_A2h, in0=b_t1, scalar1=ctx["wd_scale"][:, 0:1],
        scalar2=0.5 / (QB * RT4096), op0=ALU.mult, op1=ALU.mult),
        waits=[tg3, ctx.get("tok_down_last")], inc=S.dve)
    st["b_A2h"] = b_A2h
    st["t_q2"] = _quant(ctx, slots, b_G, t_G, n=C4)
    ctx["tok_prev_q2"] = st["t_q2"]


def _ffn_down(ctx, st, tag):
    p, S = ctx["p"], ctx["S"]
    ws = ctx["ffn"]
    psum = ctx["psum"]
    smalls = ctx["smalls"]
    pf = ctx["psum_free"]
    wslab = ws.wslab
    t_q2 = st["t_q2"]
    xs = st["xs"]
    b_A2h = st["b_A2h"]
    dn_dep = [ctx["tok_tern_wd"], t_q2]
    wst = ctx.setdefault("wslab_state", [[None, None] for _ in range(6)])
    t_res = None
    tbs = [None, None]
    for m in range(C2):
        ps = psum[4 + m % 2]
        tmm = None
        for kh in range(2):
            sl6 = ctx.get("wslab_ptr2", 0)
            ctx["wslab_ptr2"] = (sl6 + 2) % 6
            hb = wslab[:, sl6 * 1024:sl6 * 1024 + 2048]
            t_sl = p.dma("sync", hb,
                         ctx["ternd"][m][:, kh * 2048:(kh + 1) * 2048],
                         waits=dn_dep + [t for t in wst[sl6] if t] +
                               [t for t in wst[sl6 + 1] if t])
            wst[sl6] = [t_sl, wst[sl6][1]]
            wst[sl6 + 1] = [t_sl, wst[sl6 + 1][1]]
            for kk in range(C2):
                k = kh * C2 + kk
                q_c = ws.q16[:, k * SB:(k + 1) * SB]
                extra = [t_sl, t_q2] + ([pf[4 + m % 2]] if k == 0 else [])
                tmm = p.emit("tensor", lambda e, ps=ps,
                             w=hb[:, kk * 128:(kk + 1) * 128], q=q_c,
                             st=(k == 0), sp=(k == C4 - 1): e.matmul(
                    ps[:, 0:SB], w, q, start=st, stop=sp),
                    waits=extra if kk == 0 else [],
                    inc=S.pe if kk == C2 - 1 else None)
            wst[sl6] = [wst[sl6][0], tmm]
            wst[sl6 + 1] = [wst[sl6 + 1][0], tmm]
        tt_ = smalls[:, (m % 2) * SB:(m % 2) * SB + SB]
        ta = p.emit("vector", lambda e, ps=ps, o=tt_: e.tensor_tensor(
            o, ps[:, 0:SB], b_A2h, ALU.mult),
            waits=[tmm, st["t_A2h"]], inc=S.dve)
        pf[4 + m % 2] = ta
        tb = p.emit("vector", lambda e, x=xs(m), t=tt_: e.tensor_tensor(
            x, x, t, ALU.add), waits=[ta], inc=S.dve)
        tbs[m % 2] = tb
        t_res = tb
    ctx["tok_down_last"] = t_res
    ctx["tok_down_q16"] = [tmm]
    tr = ctx.setdefault("up_trtok", [None, None])
    tr[0] = tbs[0] or tr[0]
    tr[1] = tbs[1] or tr[1]


def _butterflies(ctx, slots, n, ws, eng0):
    p, S = ctx["p"], ctx["S"]
    wk = ws.wkW
    nstages = n.bit_length() - 1
    tsp = 176
    tok = None
    for st in range(nstages):
        hh = 1 << st
        g = n // (2 * hh)
        full = wk[:, 0:n * SB].rearrange("p (g two h t) -> p g two h t",
                                         g=g, two=2, h=hh)
        a_set = full[:, :, 0, :, :]
        b_set = full[:, :, 1, :, :]
        w_dep = ([tok] if tok else []) + slots.write_waits(range(n))
        t1d = p.emit("vector", lambda e, a=a_set[:, :, :, 0:tsp],
                     b=b_set[:, :, :, 0:tsp]:
                     e.tensor_tensor(b, a, b, ALU.subtract),
                     waits=w_dep, inc=S.dve)
        t1g = p.emit("gpsimd", lambda e, a=a_set[:, :, :, tsp:SB],
                     b=b_set[:, :, :, tsp:SB]:
                     e.tensor_tensor(b, a, b, ALU.subtract),
                     waits=w_dep, inc=S.gp)
        tok = p.emit("vector", lambda e, a=a_set, b=b_set:
                     e.scalar_tensor_tensor(out=a, in0=a, scalar=2.0, in1=b,
                                            op0=ALU.mult, op1=ALU.subtract),
                     waits=[t1d, t1g], inc=S.dve)
        slots.note_write(range(n), tok)
    return tok


def _absmax(ctx, slots, n, out_bc, tok_b):
    p, S = ctx["p"], ctx["S"]
    ws = ctx["ffn"]
    wk = ws.wkW
    in3 = wk[:, 0:n * SB].rearrange("p (c t) -> p t c", c=n)
    out3 = out_bc.rearrange("p (t o) -> p t o", o=1)
    t = p.emit("vector", lambda e: e.tensor_reduce(
        out3, in3, AXX, ALU.max, apply_absolute_value=True),
        waits=[tok_b] + slots.write_waits(range(n)), inc=S.dve)
    slots.note_read(range(n), t)
    tp = p.emit("gpsimd", lambda e: e.partition_all_reduce(
        out_bc, out_bc, 128, ReduceOp.absmax), waits=[t], inc=S.gp)
    return tp


def _quant(ctx, slots, b_F, t_F, n, extra=()):
    p, S = ctx["p"], ctx["S"]
    ws = ctx["ffn"]
    wk = ws.wkW
    cg = ws.cg16
    prev_tu = None
    prev_tq = [None, None]
    pend = None
    last = None
    for g in range(n // 8):
        b0 = g * 8
        grp3 = wk[:, b0 * SB:(b0 + 8) * SB].rearrange("p (c t) -> p c t", c=8)
        fb = b_F.rearrange("p (o t) -> p o t", o=1).broadcast_to([128, 8, SB])
        tu = p.emit("vector", lambda e, g3=grp3, f=fb: e.tensor_tensor(
            g3, g3, f, ALU.mult),
            waits=[t_F, prev_tu] + ctx.get("tok_prep_q16_l", []) + list(extra) +
                  slots.write_waits(range(b0, b0 + 8)), inc=S.dve)
        slots.note_write(range(b0, b0 + 8), tu)
        prev_tu = tu
        if pend is not None:
            last = _quant_tq(ctx, slots, ws, pend, prev_tq)

        def u4_(sub, b0=b0):
            return wk[:, (b0 + sub * 4) * SB:(b0 + sub * 4 + 4) * SB]

        def q4_(sub):
            return ws.qi8[:, sub * 4 * SB:(sub + 1) * 4 * SB]

        tcs = []
        for sub in range(2):
            tc = p.emit("scalar", lambda e, q=q4_(sub), u=u4_(sub):
                        e.activation(out=q, in_=u, func=ACT.Copy),
                        waits=[tu, prev_tq[sub]], inc=S.act)
            tcs.append(tc)
        t2s = []
        for sub in range(2):
            ch = cg[:, sub * 4 * SB:(sub + 1) * 4 * SB]
            tch = p.emit("vector", lambda e, o=ch, u=u4_(sub), q=q4_(sub):
                         e.scalar_tensor_tensor(out=o, in0=u, scalar=-0.5,
                                                in1=q, op0=ALU.add,
                                                op1=ALU.is_gt),
                         waits=[tcs[sub]], inc=S.dve)
            tu2 = p.emit("vector", lambda e, u=u4_(sub), q=q4_(sub):
                         e.scalar_tensor_tensor(out=u, in0=u, scalar=0.5,
                                                in1=q, op0=ALU.add,
                                                op1=ALU.is_ge),
                         waits=[tch], inc=S.dve)
            t2 = p.emit("vector", lambda e, u=u4_(sub), c=ch:
                        e.tensor_tensor(u, u, c, ALU.add),
                        waits=[tu2], inc=S.dve)
            t2s.append(t2)
        pend = (b0, t2s)
    last = _quant_tq(ctx, slots, ws, pend, prev_tq)
    return last


def _quant_tq(ctx, slots, ws, pend, prev_tq):
    p, S = ctx["p"], ctx["S"]
    wk = ws.wkW
    b0, t2s = pend
    last = None
    for sub in range(2):
        u4 = wk[:, (b0 + sub * 4) * SB:(b0 + sub * 4 + 4) * SB]
        q4 = ws.qi8[:, sub * 4 * SB:(sub + 1) * 4 * SB]
        qo4 = ws.q16[:, (b0 + sub * 4) * SB:(b0 + sub * 4 + 4) * SB]
        tq = p.emit("vector", lambda e, o=qo4, u=u4, q=q4:
                    e.scalar_tensor_tensor(out=o, in0=u, scalar=-1.0,
                                           in1=q, op0=ALU.add,
                                           op1=ALU.add),
                    waits=[t2s[sub]], inc=S.dve)
        prev_tq[sub] = tq
        rng4 = range(b0 + sub * 4, b0 + sub * 4 + 4)
        slots.note_read(rng4, tq)
        slots.note_write(rng4, tq)
        last = tq
    return last


class _scope:
    def __init__(self, nc, spec):
        self.nc = nc
        self.spec = spec
        self.cms = []

    def __enter__(self):
        class WS:
            pass
        ws = WS()
        _WS_ID[0] += 1
        i = _WS_ID[0]
        for nm, shape, dt in self.spec:
            cm = self.nc.sbuf_tensor(f"{nm}_{i}", shape, dt)
            setattr(ws, nm, cm.__enter__())
            self.cms.append(cm)
        return ws

    def __exit__(self, *a):
        for cm in reversed(self.cms):
            cm.__exit__(*a)
        return False


def emit_attn(ctx):
    nc = ctx["nc"]
    p = ctx["p"]
    ctx["ones_col"] = nc.const_aps.tensor(1.0, [1, 128])
    with _scope(nc, (
            ("kvacc", [128, 8 * 258], F32),
            ("grow", [2, T], F32),
    )) as aperm:
        ctx["aperm"] = aperm
        _emit_decay_rows(ctx)
        with _scope(nc, (
                ("awk", [128, 26 * SB], F32),
                ("abct", [128, 8 * SB], F32),
                ("acg", [128, 4 * SB], BF16),
        )) as aws:
            ctx["aws"] = aws
            _emit_ln2_spill(ctx)
            p.flush()
        with _scope(nc, (
                ("sbuf32", [128, 2 * 512], F32),
                ("sb16", [128, 4 * 512], BF16),
        )) as ss:
            _emit_wsplit(ctx, ss)
            p.flush()
        with _scope(nc, (
                ("wslabs", [128, 4 * C2 * 256], BF16),
                ("xt", [128, 2 * 2 * 2048], BF16),
                ("ksub", [128, 256], F32),
                ("vsub", [128, 258], F32),
                ("tsc", [128, 256], F32),
                ("bks", [128, 2 * 256], F32),
        )) as ws1:
            ctx["ws1"] = ws1
            _emit_phase1(ctx)
            p.flush()
        _emit_allreduce(ctx)
        with _scope(nc, (
                ("wqs", [128, 2 * C2 * 256], BF16),
                ("xc2", [128, 4 * 512], BF16),
                ("qbuf", [128, 2 * 512], F32),
                ("obuf", [128, 2 * 512], F32),
                ("opr", [128, 4 * 512], BF16),
                ("srow2", [2, 2 * 512], F32),
        )) as ws2:
            ctx["ws2"] = ws2
            _emit_phase2(ctx)
            p.flush()
        with _scope(nc, (
                ("oblk", [128, 2 * C2 * 512], BF16),
                ("wos", [128, 2 * 2 * 2048], BF16),
        )) as ws3:
            ctx["ws3"] = ws3
            _emit_wo(ctx)
            p.flush()


def _emit_decay_rows(ctx):
    p, S = ctx["p"], ctx["S"]
    ap = ctx["aperm"]
    t = p.dma("sync", ap.grow[:, :], ctx["pos_row"])
    ctx["tok_decay_rows"] = p.emit(
        "scalar", lambda e: e.activation(out=ap.grow[:, :], in_=ap.grow[:, :],
                                         func=ACT.Exp, scale=-1.0 / NSEQ),
        waits=[t], inc=S.act)


def _emit_ln2_spill(ctx):
    p, S = ctx["p"], ctx["S"]
    aws = ctx["aws"]
    xres = ctx["xres"]
    wk = aws.awk
    bct = aws.abct
    acg = aws.acg
    dep0 = [ctx["tok_resid_done"]]
    slots = SlotTracker(26)
    spill_toks = []
    prev = [[], []]
    for s in range(NSTRIPE):
        t0 = s * SB
        x_stripe = xres[:].rearrange("p (c t) -> p c t", c=C2)[:, :, t0:t0 + SB]

        sq_out = wk[:, 8 * SB:24 * SB].rearrange("p (c t) -> p c t", c=C2)
        t_sq = p.emit("scalar", lambda e, o=sq_out, i=x_stripe: e.activation(
            out=o, in_=i, func=ACT.Square),
            waits=dep0 + slots.write_waits(range(8, 24)), inc=S.act)
        slots.note_write(range(8, 24), t_sq)

        a0 = wk[:, 0:8 * SB].rearrange("p (c t) -> p c t", c=8)
        t_fx = p.emit("vector", lambda e, o=a0, a=x_stripe[:, 0:8, :],
                      b=x_stripe[:, 8:16, :]: e.tensor_tensor(o, a, b, ALU.add),
                      waits=dep0 + slots.write_waits(range(0, 8)), inc=S.dve)
        slots.note_write(range(0, 8), t_fx)

        def fold_region(base, n, tok):
            cur = n
            while cur > 1:
                hh = cur // 2
                a = wk[:, base * SB:(base + hh) * SB].rearrange(
                    "p (c t) -> p c t", c=hh)
                b = wk[:, (base + hh) * SB:(base + cur) * SB].rearrange(
                    "p (c t) -> p c t", c=hh)
                tok = p.emit("vector", lambda e, a=a, b=b:
                             e.tensor_tensor(a, a, b, ALU.add),
                             waits=[tok], inc=S.dve)
                cur = hh
            return tok

        t_sx = fold_region(0, 8, t_fx)
        slots.note_write([0], t_sx)
        slots.note_read(range(0, 8), t_sx)
        t_sxx = fold_region(8, 16, t_sq)
        slots.note_write([8], t_sxx)
        slots.note_read(range(8, 24), t_sxx)

        b_sx = bct[:, 0:SB]
        b_sxx = bct[:, SB:2 * SB]
        t1 = p.emit("gpsimd", lambda e: e.partition_all_reduce(
            b_sx, wk[:, 0:SB], 128, ReduceOp.add), waits=[t_sx], inc=S.gp)
        slots.note_read([0], t1)
        t2 = p.emit("gpsimd", lambda e: e.partition_all_reduce(
            b_sxx, wk[:, 8 * SB:9 * SB], 128, ReduceOp.add),
            waits=[t_sxx], inc=S.gp)
        slots.note_read([8], t2)

        b_mu = bct[:, 2 * SB:3 * SB]
        b_r = bct[:, 3 * SB:4 * SB]
        b_t1 = bct[:, 4 * SB:5 * SB]
        b_t2 = bct[:, 5 * SB:6 * SB]
        b_mur = bct[:, 6 * SB:7 * SB]
        tm = p.emit("vector", lambda e: e.tensor_scalar(
            out=b_mu, in0=b_sx, scalar1=1.0 / D, scalar2=None, op0=ALU.mult),
            waits=[t1], inc=S.dve)
        tv1 = p.emit("vector", lambda e: e.tensor_scalar(
            out=b_t1, in0=b_sxx, scalar1=1.0 / D, scalar2=None, op0=ALU.mult),
            waits=[t2], inc=S.dve)
        tv2 = p.emit("vector", lambda e: e.tensor_tensor(b_t2, b_mu, b_mu,
                                                         ALU.mult),
                     waits=[tm], inc=S.dve)
        tv3 = p.emit("vector", lambda e: e.scalar_tensor_tensor(
            out=b_t1, in0=b_t1, scalar=EPS_LN, in1=b_t2,
            op0=ALU.add, op1=ALU.subtract), waits=[tv1, tv2], inc=S.dve)
        ts1 = p.emit("scalar", lambda e: e.activation(
            out=b_t2, in_=b_t1, func=ACT.Sqrt), waits=[tv3], inc=S.act)
        t_r = p.emit("vector", lambda e: e.reciprocal(b_r, b_t2),
                     waits=[ts1], inc=S.dve)
        t_mur = p.emit("vector", lambda e: e.tensor_tensor(b_mur, b_mu, b_r,
                                                           ALU.mult),
                       waits=[t_r], inc=S.dve)

        for c in range(C2):
            sl = wk[:, 24 * SB:25 * SB]
            sl2 = wk[:, 25 * SB:26 * SB]
            xh = acg[:, (c % 2) * SB:(c % 2) * SB + SB]
            xl = acg[:, (2 + c % 2) * SB:(2 + c % 2) * SB + SB]
            xc = xres[:, c * T + t0:c * T + t0 + SB]
            w_prev = list(prev[c % 2])
            ta = p.emit("vector", lambda e, o=sl, x=xc: e.tensor_tensor(
                o, x, b_r, ALU.mult),
                waits=[t_r, t_mur] + slots.write_waits([24]), inc=S.dve)
            tb = p.emit("vector", lambda e, o=sl2, a=sl: e.tensor_tensor(
                o, a, b_mur, ALU.subtract),
                waits=[ta] + slots.write_waits([25]), inc=S.dve)
            slots.note_write([24], ta)
            slots.note_write([25], tb)
            th = p.emit("scalar", lambda e, o=xh, i=sl2: e.activation(
                out=o, in_=i, func=ACT.Copy), waits=[tb] + w_prev, inc=S.act)
            tl = p.emit("vector", lambda e, o=xl, i=sl2, h=xh: e.tensor_tensor(
                o, i, h, ALU.subtract), waits=[th], inc=S.dve)
            slots.note_read([25], tl)
            toks = []
            for half, buf, tok_src in ((0, xh, th), (1, xl, tl)):
                dstt = ctx["xln_h" if half == 0 else "xln_l"]
                dst = dstt[:].rearrange("p (ts kc) -> p ts kc", ts=16)[
                    :, 2 * s:2 * s + 2, c * 128:(c + 1) * 128]
                srcb = buf.rearrange("p (two t) -> p two t", two=2)
                toks.append(p.dma("sync", dst, srcb, waits=[tok_src]))
            prev[c % 2] = toks
            spill_toks.extend(toks)
    ctx["tok_xln"] = spill_toks[-4:]


def _emit_wsplit(ctx, ss):
    p, S = ctx["p"], ctx["S"]
    carry = []
    last_lo = None
    for nm, pairnm in (("wk_t", "wk_pair"), ("wv_t", "wv_pair"),
                       ("wq_t", "wq_pair"), ("wo_t", "wo_pair")):
        wt = ctx[nm]
        pair = ctx[pairnm]
        toks = list(carry)
        for it0 in range(C2 * 4):
            it = it0 + len(carry)
            k, j = divmod(it0, 4)
            sl = it0 % 2
            buf = ss.sbuf32[:, sl * 512:sl * 512 + 512]
            hbuf = ss.sb16[:, sl * 512:sl * 512 + 512]
            lbuf = ss.sb16[:, 1024 + sl * 512:1024 + sl * 512 + 512]
            w_dep = ([last_lo] if it < 2 else []) + \
                ([toks[it - 2]] if it >= 2 else [])
            tl_ = p.dma("sync", buf,
                        wt[k * 128:(k + 1) * 128, j * 512:(j + 1) * 512],
                        waits=w_dep)
            th_ = p.emit("scalar", lambda e, b=buf, h=hbuf: e.activation(
                out=h, in_=b, func=ACT.Copy),
                waits=[tl_] + ([toks[it - 2]] if it >= 2 else []), inc=S.act)
            tlo = p.emit("vector", lambda e, b=buf, h=hbuf, l=lbuf:
                         e.tensor_tensor(l, b, h, ALU.subtract),
                         waits=[th_], inc=S.dve)
            tws = []
            for hw, srcb in ((0, hbuf), (1, lbuf)):
                if nm == "wo_t":
                    src2 = srcb.rearrange("p (f c) -> p f c", f=4)
                    dst = pair[hw, j * 4:(j + 1) * 4, :,
                               k * 128:(k + 1) * 128].transpose([1, 0, 2])
                else:
                    src2 = srcb.rearrange("p (s c) -> p s c", s=2)
                    dst = pair[hw, j * 2:(j + 1) * 2, :,
                               k * 256:(k + 1) * 256].transpose([1, 0, 2])
                tws.append(p.dma("sync", dst, src2, waits=[tlo]))
            toks.append(tws[1])
            last_lo = tlo
        ctx[f"tok_{pairnm}"] = toks[-2:]
        carry = toks[-2:]


def _emit_phase1(ctx):
    p, S = ctx["p"], ctx["S"]
    ws1 = ctx["ws1"]
    psum = ctx["psum"]
    aperm = ctx["aperm"]
    pf = ctx["psum_free"]
    dep_w = ctx["tok_wk_pair"] + ctx["tok_wv_pair"]
    dep_x = ctx["tok_xln"]
    decay_col = ctx["decay_col"]
    t_ones = p.emit("vector", lambda e: e.memset(ws1.vsub[:, 128:129], 1.0),
                    inc=S.dve)
    t_ones = p.emit("vector", lambda e: e.memset(ws1.vsub[:, 257:258], 1.0),
                    waits=[t_ones], inc=S.dve)
    kv_ev = []
    xt_state = [[None, None], [None, None]]
    slab_gate = None
    bks_gate = None
    prev_kv = [None, None]
    k_gate = t_ones
    v_gate = t_ones
    for sl in range(8):
        sdep = dep_w + ([slab_gate] if slab_gate else [])
        slab_toks = []
        for qi_, (pairnm, hw) in enumerate((("wk_pair", 0), ("wk_pair", 1),
                                            ("wv_pair", 0), ("wv_pair", 1))):
            dst = ws1.wslabs[:, qi_ * 4096:(qi_ + 1) * 4096]
            slab_toks.append(p.dma("sync", dst, ctx[pairnm][hw, sl],
                                   waits=sdep))
        tbk = p.dma("sync", ws1.bks[:, 0:256],
                    ctx["bk_rep"][:, sl * 256:(sl + 1) * 256],
                    waits=[bks_gate] if bks_gate else [])
        tbv = p.dma("sync", ws1.bks[:, 256:512],
                    ctx["bv_rep"][:, sl * 256:(sl + 1) * 256], waits=[tbk])
        kvb = psum[2 + sl % 2]
        kv_last = None
        mmk = None
        for ts in range(16):
            xb = ts % 2
            xh_t = ws1.xt[:, xb * 4096:xb * 4096 + 2048]
            xl_t = ws1.xt[:, xb * 4096 + 2048:xb * 4096 + 4096]
            xdep = dep_x + [t for t in xt_state[xb] if t]
            t_xh = p.dma("sync", xh_t,
                         ctx["xln_h"][:].rearrange("p (t k) -> p t k",
                                                   t=16)[:, ts, :],
                         waits=xdep)
            t_xl = p.dma("sync", xl_t,
                         ctx["xln_l"][:].rearrange("p (t k) -> p t k",
                                                   t=16)[:, ts, :],
                         waits=[t_xh])
            psK = psum[0]
            psV = psum[1]
            for k in range(C2):
                xh_c = xh_t[:, k * 128:(k + 1) * 128]
                xl_c = xl_t[:, k * 128:(k + 1) * 128]
                wkh = ws1.wslabs[:, 0 * 4096 + k * 256:0 * 4096 + (k + 1) * 256]
                wkl = ws1.wslabs[:, 1 * 4096 + k * 256:1 * 4096 + (k + 1) * 256]
                wvh = ws1.wslabs[:, 2 * 4096 + k * 256:2 * 4096 + (k + 1) * 256]
                wvl = ws1.wslabs[:, 3 * 4096 + k * 256:3 * 4096 + (k + 1) * 256]
                w0 = []
                if k == 0:
                    w0 = [t_xh, t_xl] + slab_toks + \
                        [t for t in (pf[0], pf[1]) if t]
                st = (k == 0)
                sp = (k == C2 - 1)
                p.emit("tensor", lambda e, o=psK, x=xh_c, w=wkh, st=st:
                       e.matmul(o[:, 0:256], x, w, start=st, stop=False),
                       waits=w0)
                p.emit("tensor", lambda e, o=psK, x=xh_c, w=wkl:
                       e.matmul(o[:, 0:256], x, w, start=False, stop=False))
                p.emit("tensor", lambda e, o=psK, x=xl_c, w=wkh, sp=sp:
                       e.matmul(o[:, 0:256], x, w, start=False, stop=sp))
                p.emit("tensor", lambda e, o=psV, x=xh_c, w=wvh, st=st:
                       e.matmul(o[:, 0:256], x, w, start=st, stop=False),
                       waits=w0)
                p.emit("tensor", lambda e, o=psV, x=xh_c, w=wvl:
                       e.matmul(o[:, 0:256], x, w, start=False, stop=False))
                mmk = p.emit("tensor", lambda e, o=psV, x=xl_c, w=wvh, sp=sp:
                             e.matmul(o[:, 0:256], x, w, start=False, stop=sp),
                             inc=S.pe if sp else None)
            xt_state[xb] = [t_xl, mmk]
            tk1 = p.emit("vector", lambda e, o=ws1.tsc[:, 0:256], i=psK:
                         e.scalar_tensor_tensor(
                             out=o, in0=i[:, 0:256], scalar=1.0,
                             in1=ws1.bks[:, 0:256], op0=ALU.mult, op1=ALU.add),
                         waits=[mmk, tbk, k_gate], inc=S.dve)
            pf[0] = tk1
            tk2 = p.emit("vector", lambda e, o=ws1.ksub[:, 0:256],
                         i=ws1.tsc[:, 0:256], dc=decay_col[:, ts:ts + 1]:
                         e.tensor_scalar(out=o, in0=i, scalar1=0.0, scalar2=dc,
                                         op0=ALU.max, op1=ALU.mult),
                         waits=[tk1, ctx["tok_decay_col"], k_gate], inc=S.dve)
            vout = ws1.vsub[:].rearrange("p (h c) -> p h c", h=2)[:, :, 0:128]
            tv = p.emit("vector", lambda e, o=vout, i=psV:
                        e.scalar_tensor_tensor(
                            out=o,
                            in0=i[:, 0:256].rearrange("p (h c) -> p h c", h=2),
                            scalar=1.0,
                            in1=ws1.bks[:, 256:512].rearrange(
                                "p (h c) -> p h c", h=2),
                            op0=ALU.mult, op1=ALU.add),
                        waits=[mmk, tbv, v_gate], inc=S.dve)
            pf[1] = tv
            for h2 in range(2):
                kv_last = p.emit(
                    "tensor", lambda e, o=kvb,
                    l=ws1.ksub[:, h2 * 128:(h2 + 1) * 128],
                    r=ws1.vsub[:, h2 * 129:h2 * 129 + 129],
                    st=(ts == 0 and h2 == 0), sp=(ts == 15 and h2 == 1),
                    h2=h2:
                    e.matmul(o[:, h2 * 129:h2 * 129 + 129], l, r,
                             start=st, stop=sp),
                    waits=[tk2, tv] + ([prev_kv[sl % 2]]
                                       if ts == 0 and h2 == 0 and
                                       prev_kv[sl % 2] else []),
                    inc=S.pe if (ts == 15 and h2 == 1) else None)
            k_gate = kv_last
            v_gate = kv_last
        tke = p.emit("scalar", lambda e, i=kvb,
                     o=aperm.kvacc[:, sl * 258:(sl + 1) * 258]: e.activation(
            out=o, in_=i[:, 0:258], func=ACT.Copy), waits=[kv_last], inc=S.act)
        prev_kv[sl % 2] = tke
        kv_ev.append(tke)
        slab_gate = mmk
        bks_gate = tk1 if True else None
        bks_gate = kv_last
    ctx["tok_kv"] = kv_ev[-2:]


def _emit_allreduce(ctx):
    p, S = ctx["p"], ctx["S"]
    aperm = ctx["aperm"]
    t1 = p.dma("sync", ctx["cc_in"], aperm.kvacc[:, 0:H * 129],
               waits=ctx["tok_kv"])
    tcc = p.emit("gpsimd", lambda e: e.collective_compute(
        "AllReduce", ALU.add, ins=[ctx["cc_in"].opt()],
        outs=[ctx["cc_out"].opt()], replica_groups=[list(range(NCORES))]),
        waits=[t1], inc=S.cc)
    t2 = p.dma("sync", aperm.kvacc[:, 0:H * 129], ctx["cc_out"], waits=[tcc])
    ctx["tok_kv_all"] = t2


def _emit_phase2(ctx):
    p, S = ctx["p"], ctx["S"]
    ws2 = ctx["ws2"]
    psum = ctx["psum"]
    aperm = ctx["aperm"]
    pf = ctx["psum_free"]
    dep_x = ctx["tok_xln"]
    kv_t = ctx["tok_kv_all"]
    NB = 4
    BW = 512
    wq_gate = None
    xc_state = [[None, None], [None, None]]
    o_sp = []
    pv = {}
    for e8 in range(8):
        sdep = ctx["tok_wq_pair"] + ([wq_gate] if wq_gate else [])
        tq1 = p.dma("sync", ws2.wqs[:, 0:4096], ctx["wq_pair"][0, e8],
                    waits=sdep)
        tq2 = p.dma("sync", ws2.wqs[:, 4096:8192], ctx["wq_pair"][1, e8],
                    waits=[tq1])
        for b in range(NB):
            psQ = [psum[0], psum[1]]
            mml = None
            for k in range(C2):
                xb = k % 2
                xh_c = ws2.xc2[:, xb * 1024:xb * 1024 + 512]
                xl_c = ws2.xc2[:, xb * 1024 + 512:xb * 1024 + 1024]
                xdep = dep_x + [t for t in xc_state[xb] if t]
                src_h = ctx["xln_h"][:].rearrange(
                    "p (t kc) -> p t kc", t=16)[:, 4 * b:4 * b + 4,
                                                k * 128:(k + 1) * 128]
                src_l = ctx["xln_l"][:].rearrange(
                    "p (t kc) -> p t kc", t=16)[:, 4 * b:4 * b + 4,
                                                k * 128:(k + 1) * 128]
                th_ = p.dma("sync", xh_c.rearrange("p (f c) -> p f c", f=4),
                            src_h, waits=xdep)
                tl_ = p.dma("sync", xl_c.rearrange("p (f c) -> p f c", f=4),
                            src_l, waits=[th_])
                for h2 in range(2):
                    wqh = ws2.wqs[:, k * 256 + h2 * 128:
                                  k * 256 + (h2 + 1) * 128]
                    wql = ws2.wqs[:, 4096 + k * 256 + h2 * 128:
                                  4096 + k * 256 + (h2 + 1) * 128]
                    w0 = [th_, tl_, tq1, tq2]
                    if k == 0:
                        w0 = w0 + [t for t in (pf[0], pf[1]) if t]
                    st = (k == 0)
                    sp = (k == C2 - 1)
                    p.emit("tensor", lambda e, o=psQ[h2], w=wqh, x=xh_c, st=st:
                           e.matmul(o[:, 0:BW], w, x, start=st, stop=False),
                           waits=w0)
                    p.emit("tensor", lambda e, o=psQ[h2], w=wqh, x=xl_c:
                           e.matmul(o[:, 0:BW], w, x, start=False, stop=False))
                    mml = p.emit("tensor", lambda e, o=psQ[h2], w=wql, x=xh_c,
                                 sp=sp: e.matmul(o[:, 0:BW], w, x,
                                                 start=False, stop=sp),
                                 inc=S.pe if (sp and h2 == 1) else None)
                xc_state[xb] = [tl_, mml]
            for h2 in range(2):
                h = e8 * 2 + h2
                qb = ws2.qbuf[:, h2 * 512:(h2 + 1) * 512]
                tqe = p.emit("scalar", lambda e, o=qb, i=psQ[h2],
                             bq=ctx["bq_col"][:, h:h + 1]: e.activation(
                    out=o, in_=i[:, 0:BW], func=ACT.Relu, bias=bq, scale=1.0),
                    waits=[mml, pv.get(("q", h2))] + ctx["tok_bias_list"],
                    inc=S.act)
                pf[h2] = tqe
                ops_ = psum[2 + h2]
                kvl = aperm.kvacc[:, e8 * 258 + h2 * 129:
                                  e8 * 258 + h2 * 129 + 128]
                to_ = p.emit("tensor", lambda e, o=ops_, l=kvl, r=qb:
                             e.matmul(o[:, 0:BW], l, r, start=True, stop=True),
                             waits=[tqe, kv_t, pv.get(("o", h2))],
                             inc=S.pe)
                sps = psum[4 + h2]
                ksl = aperm.kvacc[:, e8 * 258 + h2 * 129 + 128:
                                  e8 * 258 + h2 * 129 + 129]
                ts_ = p.emit("tensor", lambda e, o=sps, l=ksl, r=qb:
                             e.matmul(o[0:1, 0:BW], l, r, start=True,
                                      stop=True),
                             waits=[tqe, kv_t, pv.get(("s", h2))],
                             inc=S.pe)
                srow = ws2.srow2[0:1, h2 * 512:(h2 + 1) * 512]
                tse = p.emit("scalar", lambda e, o=srow, i=sps: e.activation(
                    out=o, in_=i[0:1, 0:BW], func=ACT.Copy),
                    waits=[ts_, pv.get(("sr", h2))], inc=S.act)
                grow = aperm.grow[0:1, b * BW:(b + 1) * BW]
                tg1 = p.emit("vector", lambda e, o=srow, d=grow:
                             e.tensor_tensor(o, o, d, ALU.mult),
                             waits=[tse, ctx["tok_decay_rows"]], inc=S.dve)
                tg2 = p.emit("vector", lambda e, o=srow: e.tensor_scalar(
                    out=o, in0=o, scalar1=EPS_ATTN, scalar2=None, op0=ALU.add),
                    waits=[tg1], inc=S.dve)
                tg3 = p.emit("vector", lambda e, o=srow: e.reciprocal(o, o),
                             waits=[tg2], inc=S.dve)
                tg4 = p.emit("vector", lambda e, o=srow, d=grow:
                             e.tensor_tensor(o, o, d, ALU.mult),
                             waits=[tg3], inc=S.dve)
                pv[("sr", h2)] = tg4
                gbps = psum[6 + h2]
                tb_ = p.emit("tensor", lambda e, o=gbps, l=ctx["ones_col"],
                             r=srow: e.matmul(o[:, 0:BW], l, r, start=True,
                                              stop=True),
                             waits=[tg4, pv.get(("gb", h2))], inc=S.pe)
                gbs = ws2.obuf[:, h2 * 512:(h2 + 1) * 512]
                tbe = p.emit("scalar", lambda e, o=gbs, i=gbps: e.activation(
                    out=o, in_=i[:, 0:BW], func=ACT.Copy),
                    waits=[tb_, pv.get(("gbs", h2))], inc=S.act)
                pv[("gb", h2)] = tbe
                oh = ws2.opr[:, h2 * 1024:h2 * 1024 + 512]
                ol = ws2.opr[:, h2 * 1024 + 512:h2 * 1024 + 1024]
                tof = p.emit("vector", lambda e, o=qb, a=ops_, g=gbs:
                             e.tensor_tensor(o, a[:, 0:BW], g, ALU.mult),
                             waits=[to_, ts_, tbe], inc=S.dve)
                pv[("o", h2)] = tof
                pv[("q", h2)] = tof
                pv[("gbs", h2)] = tof
                toh = p.emit("scalar", lambda e, o=oh, i=qb: e.activation(
                    out=o, in_=i, func=ACT.Copy),
                    waits=[tof, pv.get(("oh", h2))], inc=S.act)
                tol = p.emit("vector", lambda e, o=ol, i=qb, h_=oh:
                             e.tensor_tensor(o, i, h_, ALU.subtract),
                             waits=[toh], inc=S.dve)
                t1 = p.dma("sync", ctx["oj_h"][:, h * T + b * BW:
                                               h * T + (b + 1) * BW], oh,
                           waits=[toh])
                t2 = p.dma("sync", ctx["oj_l"][:, h * T + b * BW:
                                               h * T + (b + 1) * BW], ol,
                           waits=[tol])
                pv[("oh", h2)] = t2
                o_sp.extend([t1, t2])
        wq_gate = mml
    ctx["tok_o"] = o_sp[-4:]


def _emit_wo(ctx):
    p, S = ctx["p"], ctx["S"]
    ws3 = ctx["ws3"]
    psum = ctx["psum"]
    xres = ctx["xres"]
    pf = ctx["psum_free"]
    NB = 4
    BW = 512
    dep_o = ctx["tok_o"]
    dep_w = ctx["tok_wo_pair"]
    ob_gate = None
    wo_state = [[None, None], [None, None]]
    t_last = None
    for b in range(NB):
        src_h = ctx["oj_h"][:].rearrange("p (k t) -> p k t", k=C2)[
            :, :, b * BW:(b + 1) * BW]
        src_l = ctx["oj_l"][:].rearrange("p (k t) -> p k t", k=C2)[
            :, :, b * BW:(b + 1) * BW]
        ob_dep = dep_o + ([ob_gate] if ob_gate else [])
        t_oh = p.dma("sync", ws3.oblk[:, 0:C2 * BW].rearrange(
            "p (k t) -> p k t", k=C2), src_h, waits=ob_dep)
        t_ol = p.dma("sync", ws3.oblk[:, C2 * BW:2 * C2 * BW].rearrange(
            "p (k t) -> p k t", k=C2), src_l, waits=[t_oh])
        mml = None
        for f in range(C2):
            wb = f % 2
            wsl_h = ws3.wos[:, wb * 4096:wb * 4096 + 2048]
            wsl_l = ws3.wos[:, wb * 4096 + 2048:wb * 4096 + 4096]
            wdep = dep_w + [t for t in wo_state[wb] if t]
            tw1 = p.dma("sync", wsl_h, ctx["wo_pair"][0, f], waits=wdep)
            tw2 = p.dma("sync", wsl_l, ctx["wo_pair"][1, f], waits=[tw1])
            ps = psum[f % 2]
            for k in range(C2):
                oh_c = ws3.oblk[:, k * BW:(k + 1) * BW]
                ol_c = ws3.oblk[:, C2 * BW + k * BW:C2 * BW + (k + 1) * BW]
                wh = wsl_h[:, k * 128:(k + 1) * 128]
                wl = wsl_l[:, k * 128:(k + 1) * 128]
                w0 = []
                if k == 0:
                    w0 = [tw1, tw2, t_oh, t_ol] + \
                        ([pf[f % 2]] if pf[f % 2] else [])
                st = (k == 0)
                sp = (k == C2 - 1)
                p.emit("tensor", lambda e, o=ps, w=wh, x=oh_c, st=st:
                       e.matmul(o[:, 0:BW], w, x, start=st, stop=False),
                       waits=w0)
                p.emit("tensor", lambda e, o=ps, w=wh, x=ol_c:
                       e.matmul(o[:, 0:BW], w, x, start=False, stop=False))
                mml = p.emit("tensor", lambda e, o=ps, w=wl, x=oh_c, sp=sp:
                             e.matmul(o[:, 0:BW], w, x, start=False, stop=sp),
                             inc=S.pe if sp else None)
            wo_state[wb] = [tw2, mml]
            xsl = xres[:, f * T + b * BW:f * T + (b + 1) * BW]
            t_last = p.emit("vector", lambda e, x=xsl, i=ps,
                            bo=ctx["bo_col"][:, f:f + 1]:
                            e.scalar_tensor_tensor(out=x, in0=i[:, 0:BW],
                                                   scalar=bo, in1=x,
                                                   op0=ALU.add, op1=ALU.add),
                            waits=[mml, ctx["tok_resid_done"]] +
                                  ctx["tok_bias_list"], inc=S.dve)
            pf[f % 2] = t_last
        ob_gate = mml
    ctx["tok_attn_done"] = t_last


def emit_output(ctx):
    p, S = ctx["p"], ctx["S"]
    dep = [d for d in (ctx.get("tok_resid_done"), ctx.get("tok_attn_done"))
           if d]
    st_toks = []
    for i in range(4):
        sl = slice(i * (C2 * T // 4), (i + 1) * (C2 * T // 4))
        st_toks.append(p.dma("sync", ctx["y_out"][:, sl], ctx["xres"][:, sl],
                             waits=dep))
    p.wait_only("sync", st_toks)
    p.wait_only("gpsimd", st_toks)



def make_inputs(full):
    x = np.asarray(full["L0_attn_x"])[0]
    h128 = _hadamard(128)
    shared = {
        "w_up_t": np.ascontiguousarray(np.asarray(full["L0_ffn_w_up"]).T),
        "w_down_t": np.ascontiguousarray(np.asarray(full["L0_ffn_w_down"]).T),
        "wq_t": np.ascontiguousarray(np.asarray(full["L0_attn_Wq"]).T),
        "wk_t": np.ascontiguousarray(np.asarray(full["L0_attn_Wk"]).T),
        "wv_t": np.ascontiguousarray(np.asarray(full["L0_attn_Wv"]).T),
        "wo_t": np.ascontiguousarray(np.asarray(full["L0_attn_Wo"]).T),
        "bq_col": col_layout(full["L0_attn_bq"]),
        "bo_col": col_layout(full["L0_attn_bo"]),
        "bk_rep": np.ascontiguousarray(
            np.broadcast_to(np.asarray(full["L0_attn_bk"]), (128, D))),
        "bv_rep": np.ascontiguousarray(
            np.broadcast_to(np.asarray(full["L0_attn_bv"]), (128, D))),
        "h128_in": h128,
    }
    maps = []
    for c in range(NCORES):
        pos = np.arange(c * T, (c + 1) * T, dtype=np.float32)
        m = dict(shared)
        m["x_fm"] = fm_layout(x[c * T:(c + 1) * T])
        m["pos_col"] = np.ascontiguousarray(pos.reshape(C2, 128).T)
        m["pos_row"] = np.ascontiguousarray(np.broadcast_to(pos, (2, T)).copy())
        maps.append(m)
    return maps


def kernel(**inputs):
    nc = build_program()
    maps = make_inputs(inputs)
    res = run_bass_kernel_spmd(nc, maps, core_ids=list(range(NCORES)))
    outs = [fm_unlayout(res.results[c]["y_fm"], T, D) for c in range(NCORES)]
    return np.concatenate(outs, axis=0)[None].astype(np.float32)
